# version 12
# baseline (speedup 1.0000x reference)
import numpy as np
import ml_dtypes
import concourse.bass as bass
import concourse.mybir as mybir
import bass_rust as _br
from concourse.bass_utils import run_bass_kernel_spmd
from contextlib import ExitStack

F32 = mybir.dt.float32
BF16 = mybir.dt.bfloat16
OP = mybir.AluOpType
AF = mybir.ActivationFunctionType
AX = mybir.AxisListType

M32 = 12582912.0
M64 = 6755399441055744.0

D = 1024
INNER = 4096
SEG = 512
LNEPS = 1e-5
ZEPS = 1e-6
QEPS = 1e-5

ENGINES = ("sync", "tensor", "vector", "scalar", "gpsimd")
ENG_CTR = {"sync": "sS", "tensor": "tS", "vector": "vS", "scalar": "cS",
           "gpsimd": "gS"}
DMA_CTR = {"sync": "qsx", "gpsimd": "qgp", "scalar": "qsc"}
NLANES = 6
DMA_LANES = [f"{q}{l}" for q in ("qsx", "qgp", "qsc") for l in range(NLANES)]
ALL_CTRS = tuple(DMA_LANES) + ("sS", "tS", "vS", "cS", "gS")


class Sched:
    def __init__(self):
        self.ops = {e: [] for e in ENGINES}
        self.ctr = {c: 0 for c in ALL_CTRS}
        self.dma_i = {"qsx": 0, "qgp": 0, "qsc": 0}
        self.res = {}
        self.mark = {e: {} for e in ENGINES}
        self.pending_join = {e: None for e in ENGINES}

    def _r(self, name):
        if name not in self.res:
            self.res[name] = {"w": None, "r": []}
        return self.res[name]

    def add(self, eng, fn, reads=(), writes=(), waits=(), dma=False):
        if dma:
            q = DMA_CTR[eng]
            i = self.dma_i[q]
            self.dma_i[q] = i + 1
            cname = f"{q}{i % NLANES}"
            step = 16
            lane_reuse = (cname, 16 * (i // NLANES)) if i >= NLANES else None
        else:
            cname = ENG_CTR[eng]
            step = 1
            lane_reuse = None
        wl = {}

        def need(dep):
            if dep is None:
                return
            c, v = dep
            if v <= 0 or self.mark[eng].get(c, 0) >= v:
                return
            wl[c] = max(wl.get(c, 0), v)

        if lane_reuse is not None:
            need(lane_reuse)
        jw = self.pending_join[eng]
        if jw:
            for c, v in jw.items():
                need((c, v))
            self.pending_join[eng] = None
        for c, v in waits:
            need((c, v))
        for rn in reads:
            need(self._r(rn)["w"])
        for wn in writes:
            r = self._r(wn)
            need(r["w"])
            for dep in r["r"]:
                need(dep)
        self.ctr[cname] += step
        me = (cname, self.ctr[cname])
        for rn in reads:
            self._r(rn)["r"].append(me)
        for wn in writes:
            r = self._r(wn)
            r["w"] = me
            r["r"] = []
        for c, v in wl.items():
            assert v <= self.ctr[c], f"future wait {c}={v} now={self.ctr[c]}"
            self.mark[eng][c] = max(self.mark[eng].get(c, 0), v)
        self.ops[eng].append((tuple(sorted(wl.items())), fn, cname, step))
        return me

    def join(self):
        snap = dict(self.ctr)
        for e in ENGINES:
            self.pending_join[e] = dict(snap)
            self.mark[e] = {}

    def emit(self, nc, sems):
        with nc.Block() as blk:
            for eng in ENGINES:
                ops = self.ops[eng]

                def body(E, ops=ops):
                    for waits, fn, cname, step in ops:
                        for (c, v) in waits:
                            E.wait_ge(sems[c], v)
                        inst = fn(E)
                        inst.then_inc(sems[cname], step)
                getattr(blk, eng)(body)


def ap(t, off, dims):
    p = [list(d) for d in t.ap][0]
    return bass.AP(t.tensor, t.offset + off, [p] + [list(d) for d in dims])


def build_program(T=4096, debug_taps=()):
    NSEG = T // SEG
    NT128 = T // 128
    NSTRIP = T // 512
    NHSTRIP = T // 256

    nc = bass.Bass("TRN2", target_bir_lowering=False, debug=False)

    x_d = nc.dram_tensor("x", [T, D], F32, kind="ExternalInput").ap()
    wq_d = nc.dram_tensor("wqT", [128, 8 * D], F32, kind="ExternalInput").ap()
    wk_d = nc.dram_tensor("wkT", [128, 8 * D], F32, kind="ExternalInput").ap()
    wv_d = nc.dram_tensor("wvT", [128, 8 * D], F32, kind="ExternalInput").ap()
    wo_d = nc.dram_tensor("woT", [128, 8 * D], F32, kind="ExternalInput").ap()
    wg_d = nc.dram_tensor("wgT", [32, 128, 8 * 128], F32, kind="ExternalInput").ap()
    wu_d = nc.dram_tensor("wuT", [32, 128, 8 * 128], F32, kind="ExternalInput").ap()
    wd_d = nc.dram_tensor("wdT", [32, 128, D], F32, kind="ExternalInput").ap()
    out_d = nc.dram_tensor("out", [T, D], F32, kind="ExternalOutput").ap()
    x1_d = nc.dram_tensor("x1_scr", [T, D], F32, kind="Internal").ap()
    h_d = nc.dram_tensor("h_scr", [NHSTRIP, INNER, 256], BF16, kind="Internal").ap()

    taps = {}

    def tap(name, shape, dtype=F32):
        if name in debug_taps:
            taps[name] = nc.dram_tensor("tap_" + name, shape, dtype,
                                        kind="ExternalOutput").ap()
        return taps.get(name)

    t_xn = tap("xn", [128, 8 * SEG], BF16)
    t_phiq = tap("phiq", [128, 8 * SEG], BF16)
    t_phik = tap("phik", [128, 8 * SEG], BF16)
    t_vaug = tap("vaug", [4, 128, 8 * 132], BF16)
    t_y = tap("y", [128, 8 * SEG], BF16)
    t_xq = tap("xq", [128, 8 * T], BF16)
    t_wdq = tap("wdq", [128, 32 * D], BF16)
    t_mx = tap("mx", [128, NT128])

    ctx = ExitStack()
    sb = lambda name, shape, dt=F32: ctx.enter_context(
        nc.sbuf_tensor(name, shape, dt)).ap()
    psb = lambda name, shape, dt=F32: ctx.enter_context(
        nc.psum_tensor(name, shape, dt)).ap()

    iot = sb("iot", [128, 128])
    ident = sb("ident", [128, 128], BF16)
    identf = sb("identf", [128, 128])
    mask2 = sb("mask2", [128, 256], BF16)
    onecol = sb("onecol", [128, 1])
    epscol = sb("epscol", [128, 1])
    nhalfcol = sb("nhalfcol", [128, 1])
    onerow_f = sb("onerow_f", [1, 128])
    onesmat = sb("onesmat", [128, 128])
    disc = sb("disc", [128, 1])
    dprime = sb("dprime", [128, 1])
    mw = sb("mw", [128, 4])
    thr = sb("thr", [128, 6])
    wsum = sb("wsum", [128, 4])
    redcol = sb("redcol", [128, 1])
    bigacc = sb("bigacc", [128, 384])
    colsA = sb("colsA", [128, 32])
    colsQ = sb("colsQ", [128, 64])
    rdc = sb("rdc", [128, 8])
    mxw = sb("mxw", [128, NT128])
    qrow1 = [sb(f"qrow1_{i}", [1, 128]) for i in range(2)]
    betac = sb("betac", [128, NT128])

    A1 = sb("arena1", [128, 4 * 8 * D], BF16)
    A2 = sb("arena2", [128, 22528], BF16)
    A3 = sb("arena3", [128, 7424])
    XQ = sb("xq_arena", [128, max(8 * T, 24576)], BF16)

    wq_sb = A1[:, 0:8192]
    wk_sb = A1[:, 8192:16384]
    wv_sb = A1[:, 16384:24576]
    wo_sb = A1[:, 24576:32768]
    wdq_sb = A1[:, 0:32768]

    xnT = A2[:, 0:4096]
    phiQ = A2[:, 4096:8192]
    phiK = A2[:, 8192:12288]
    yT = A2[:, 0:4096]
    vaug = A2[:, 12288:16512]
    xn_tm = [A2[:, 16512 + i * 1024:16512 + (i + 1) * 1024] for i in range(2)]
    attnT = [A2[:, 18560 + i * 256:18560 + (i + 1) * 256] for i in range(2)]
    phiKT = [A2[:, 19072 + i * 128:19072 + (i + 1) * 128] for i in range(2)]
    sbf3 = [A2[:, 19328 + i * 396:19328 + (i + 1) * 396] for i in range(3)]
    y_tm = [A2[:, 20516 + i * 128:20516 + (i + 1) * 128] for i in range(2)]
    ptmp2 = [A2[:, 20772 + i * 512:20772 + (i + 1) * 512] for i in range(2)]
    maxacc = A2[:, 0:T]
    minacc = A2[:, T:2 * T]
    wqg_sb = [A2[:, 8192 + i * 1024:8192 + (i + 1) * 1024] for i in range(2)]
    wqu_sb = [A2[:, 10240 + i * 1024:10240 + (i + 1) * 1024] for i in range(2)]
    g2_sb = [A2[:, 12288 + i * 512:12288 + (i + 1) * 512] for i in range(2)]
    ud_sb = [A2[:, 13312 + i * 512:13312 + (i + 1) * 512] for i in range(2)]
    hbf_sb = [A2[:, 14336 + i * 512:14336 + (i + 1) * 512] for i in range(2)]
    mxabs = A2[:, 18432:18432 + T]

    xseg = A3[:, 0:4096]
    ptmp = [A3[:, 4096 + i * 512:4096 + (i + 1) * 512] for i in range(2)]
    meanbuf = [A3[:, 5376 + i * 256:5376 + (i + 1) * 256] for i in range(4)]
    sqout = A3[:, 6400:7424]
    wstream = [A3[:, i * 2048:(i + 1) * 2048] for i in range(2)]
    wdbuf = [A3[:, 4096 + i * 1024:4096 + (i + 1) * 1024] for i in range(2)]
    meanbig = [A3[:, 5376 + i * 512:5376 + (i + 1) * 512] for i in range(2)]
    qy = A3[:, 0:1024]
    qC = A3[:, 1024:2048]
    x1ld = [A3[:, 2048 + i * 1024:2048 + (i + 1) * 1024] for i in range(2)]
    outt = [A3[:, 4096 + i * 1024:4096 + (i + 1) * 1024] for i in range(2)]
    qsb = [A3[:, 6144 + i * 128:6144 + (i + 1) * 128] for i in range(2)]

    xq_sb = XQ[:, 0:8 * T]
    hstrip = [XQ[:, i * 8192:(i + 1) * 8192] for i in range(2)]
    hq_sb = [XQ[:, 16384 + i * 4096:16384 + (i + 1) * 4096] for i in range(2)]

    psB = [psb(f"psB{i}", [128, 512]) for i in range(7)]
    psT = psb("psT", [128, 1024], BF16)

    sems = {}
    for n in ALL_CTRS:
        sems[n] = ctx.enter_context(nc.semaphore("sem_" + n))

    S = Sched()

    S.add("gpsimd", lambda E: E.iota(iot, [[1, 128]], channel_multiplier=-1,
                                     allow_small_or_imprecise_dtypes=True),
          writes=["iot"])
    S.add("vector", lambda E: E.tensor_single_scalar(ident, iot, 0.0, OP.is_equal),
          reads=["iot"], writes=["ident"])
    S.add("vector", lambda E: E.tensor_single_scalar(identf, iot, 0.0, OP.is_equal),
          reads=["iot"], writes=["identf"])
    S.add("vector", lambda E: E.tensor_single_scalar(
        ap(mask2, 0, [[128, 2], [1, 128]]),
        ap(iot, 0, [[0, 2], [1, 128]]), 0.0, OP.is_ge),
        reads=["iot"], writes=["mask2"])
    S.add("vector", lambda E: E.memset(onecol, 1.0), writes=["onecol"])
    S.add("vector", lambda E: E.memset(epscol, LNEPS), writes=["epscol"])
    S.add("vector", lambda E: E.memset(nhalfcol, -0.5), writes=["nhalfcol"])
    S.add("vector", lambda E: E.memset(onerow_f, 1.0), writes=["onerow"])
    S.add("vector", lambda E: E.memset(onesmat, 1.0), writes=["onesmat"])
    S.add("vector", lambda E: E.tensor_scalar(disc, onecol, M64, M64,
                                              OP.add, OP.subtract),
          reads=["onecol"], writes=["disc"])
    S.add("vector", lambda E: E.tensor_scalar(dprime, disc, -1.0, 1.0,
                                              OP.mult, OP.add),
          reads=["disc"], writes=["dprime"])
    for b in range(3):
        S.add("vector", lambda E, b=b: E.memset(psB[b], 0.0), writes=[f"B{b}"])
    S.add("vector", lambda E: E.memset(vaug, 0.0), writes=["vaug"])
    S.add("vector", lambda E: E.memset(
        ap(vaug, 64, [[1056, 4], [132, 8], [66, 2]]), 1.0),
        writes=["vaug"])

    for (dst, src, nm) in ((wq_sb, wq_d, "wq"), (wk_sb, wk_d, "wk"),
                           (wv_sb, wv_d, "wv"), (wo_sb, wo_d, "wo")):
        S.add("gpsimd", lambda E, d=dst, s=src: E.dma_start(out=d, in_=s),
              writes=[nm], dma=True)

    S.add("vector", lambda E: E.memset(bigacc, 0.0), writes=["bigacc"])
    mean_srcs = []
    for oc in range(32):
        mean_srcs.append((wg_d, oc * 128 * 1024))
    for oc in range(32):
        mean_srcs.append((wu_d, oc * 128 * 1024))
    for jc in range(32):
        mean_srcs.append((wd_d, jc * 128 * 1024))
    mean_srcs = mean_srcs[0:32][::1] + mean_srcs[32:64] + mean_srcs[64:96]
    _prep = {"i": 0}

    def emit_prepass(n):
        while n > 0 and _prep["i"] < 192:
            i = _prep["i"]
            _prep["i"] += 1
            n -= 1
            srct, off = mean_srcs[i // 2]
            half = i % 2
            b = i % 2
            sub = bass.AP(srct.tensor, off + half * 512, [[1024, 128], [1, 512]])
            S.add("sync", lambda E, b=b, s=sub: E.dma_start(out=meanbig[b], in_=s),
                  writes=[f"meanbig{b}"], dma=True)
            S.add("scalar", lambda E, b=b, i=i: E.activation(
                meanbig[b], meanbig[b], AF.Abs, accum_out=bigacc[:, i:i + 1]),
                reads=[f"meanbig{b}", "bigacc"], writes=[f"meanbig{b}", f"bigacc{i}"])

    def emit_prepass_final():
        for m in range(3):
            S.add("tensor", lambda E, m=m: E.matmul(
                psB[6][:, 0:64], onesmat, bigacc[:, m * 64:(m + 1) * 64],
                start=True, stop=True),
                reads=["onesmat"] + [f"bigacc{i}" for i in range(m * 64, (m + 1) * 64)],
                writes=["B6"])
            S.add("vector", lambda E, m=m: E.tensor_reduce(
                mw[:, m:m + 1], psB[6][:, 0:64], AX.X, OP.add),
                reads=["B6"], writes=["mwraw"])
        S.add("vector", lambda E: E.tensor_scalar(mw[:, 0:3], mw[:, 0:3],
                                                  1.0 / (INNER * D), QEPS,
                                                  OP.mult, OP.max),
              reads=["mwraw"], writes=["mw"])
        S.add("vector", lambda E: E.tensor_scalar(
            ap(thr, 0, [[2, 3], [1, 1]]), mw[:, 0:3], 0.5, None, OP.mult),
            reads=["mw"], writes=["thr"])
        S.add("vector", lambda E: E.tensor_scalar(
            ap(thr, 1, [[2, 3], [1, 1]]), mw[:, 0:3], -0.5, None, OP.mult),
            reads=["mw"], writes=["thr"])

    PAIR_BANK = {0: 0, 1: 0, 2: 0, 3: 1, 4: 1, 5: 1, 6: 2, 7: 2}
    PAIR_SLOT = {0: 0, 1: 1, 2: 2, 3: 0, 4: 1, 5: 2, 6: 0, 7: 1}
    first_S_mm = [True, True, True]

    def qc(tt, k):
        return colsQ[:, tt * 16 + k:tt * 16 + k + 1]

    def x_load(s, tt):
        S.add("sync", lambda E, s=s, tt=tt: E.dma_start(
            out=xseg[:, tt * D:(tt + 1) * D],
            in_=bass.AP(x_d.tensor, (s * SEG + tt * 128) * D,
                        [[D, 128], [1, D]])),
            writes=[f"xseg{tt}"], dma=True)

    def ln1_reduce(s, tt):
        xt = xseg[:, tt * D:(tt + 1) * D]
        S.add("scalar", lambda E, xt=xt, tt=tt: E.activation(
            sqout, xt, AF.Square, accum_out=colsA[:, 4 + tt:5 + tt]),
            reads=[f"xseg{tt}"], writes=["sqout", "cA_sq"])
        S.add("vector", lambda E, xt=xt, tt=tt: E.tensor_reduce(
            colsA[:, tt:tt + 1], xt, AX.X, OP.add),
            reads=[f"xseg{tt}"], writes=["cA_sum"])

    def ln1_cols_xn(s):
        S.add("vector", lambda E: E.tensor_scalar(
            colsA[:, 8:12], colsA[:, 0:4], 1.0 / D, None, OP.mult),
            reads=["cA_sum"], writes=["cA_m"])
        S.add("vector", lambda E: E.tensor_tensor(
            colsA[:, 12:16], colsA[:, 8:12], colsA[:, 8:12], OP.mult),
            reads=["cA_m"], writes=["cA_m2"])
        S.add("vector", lambda E: E.scalar_tensor_tensor(
            colsA[:, 16:20], colsA[:, 4:8], 1.0 / D, colsA[:, 12:16],
            OP.mult, OP.subtract),
            reads=["cA_sq", "cA_m2"], writes=["cA_var"])
        S.add("scalar", lambda E: E.activation(
            colsA[:, 20:24], colsA[:, 16:20], AF.Ln, bias=epscol),
            reads=["cA_var", "epscol"], writes=["cA_ln"])
        S.add("scalar", lambda E: E.activation(
            colsA[:, 24:28], colsA[:, 20:24], AF.Exp, scale=nhalfcol),
            reads=["cA_ln", "nhalfcol"], writes=["cA_rstd"])
        S.add("vector", lambda E: E.scalar_tensor_tensor(
            colsA[:, 28:32], colsA[:, 8:12], -1.0, colsA[:, 24:28],
            OP.mult, OP.mult),
            reads=["cA_m", "cA_rstd"], writes=["cA_mb"])
        for tt in range(4):
            xt = xseg[:, tt * D:(tt + 1) * D]
            S.add("vector", lambda E, xt=xt, tt=tt: E.tensor_scalar(
                xn_tm[tt % 2], xt, colsA[:, 24 + tt:25 + tt],
                colsA[:, 28 + tt:29 + tt], OP.mult, OP.add),
                reads=[f"xseg{tt}", "cA_rstd", "cA_mb"], writes=[f"xntm{tt % 2}"])
            for g in range(2):
                for i in range(4):
                    ic = g * 4 + i
                    S.add("tensor", lambda E, tt=tt, ic=ic, i=i: E.transpose(
                        psT[:, i * 256:i * 256 + 128],
                        xn_tm[tt % 2][:, ic * 128:(ic + 1) * 128], ident),
                        reads=[f"xntm{tt % 2}", "ident"], writes=["psT"])
                S.add("scalar", lambda E, tt=tt, g=g: E.copy(
                    ap(xnT, g * 4 * 512 + tt * 128, [[512, 4], [1, 128]]),
                    ap(psT, 0, [[256, 4], [1, 128]])),
                    reads=["psT"], writes=["xnT"])

    def qkv_proj(s):
        for (which, wsb, wres, phidst, phin) in (
                ("q", wq_sb, "wq", phiQ, "phiQ"), ("k", wk_sb, "wk", phiK, "phiK")):
            for oc in range(8):
                bank = 3 + (oc % 4)
                for kc in range(8):
                    S.add("tensor", lambda E, oc=oc, kc=kc, bank=bank, wsb=wsb: E.matmul(
                        psB[bank], wsb[:, kc * D + oc * 128:kc * D + oc * 128 + 128],
                        xnT[:, kc * 512:(kc + 1) * 512],
                        start=(kc == 0), stop=(kc == 7)),
                        reads=[wres, "xnT"], writes=[f"B{bank}"])
                S.add("scalar", lambda E, oc=oc, bank=bank: E.activation(
                    ptmp[oc % 2], psB[bank], AF.Exp),
                    reads=[f"B{bank}"], writes=[f"ptmp{oc % 2}"])
                S.add("scalar", lambda E, oc=oc, bank=bank: E.activation(
                    ptmp2[oc % 2], psB[bank], AF.Relu),
                    reads=[f"B{bank}"], writes=[f"ptmp2{oc % 2}"])
                S.add("vector", lambda E, oc=oc, phidst=phidst, phin=phin: E.scalar_tensor_tensor(
                    phidst[:, oc * 512:(oc + 1) * 512], ptmp[oc % 2], 1.0,
                    ptmp2[oc % 2], OP.min, OP.add),
                    reads=[f"ptmp{oc % 2}", f"ptmp2{oc % 2}"], writes=[phin])
        for tt in range(4):
            for oh in range(2):
                bank = 3 + ((2 * tt + oh) % 4)
                for ic in range(8):
                    S.add("tensor", lambda E, tt=tt, oh=oh, ic=ic, bank=bank: E.matmul(
                        psB[bank], xnT[:, ic * 512 + tt * 128:ic * 512 + (tt + 1) * 128],
                        wv_sb[:, ic * D + oh * 512:ic * D + (oh + 1) * 512],
                        start=(ic == 0), stop=(ic == 7)),
                        reads=["xnT", "wv"], writes=[f"B{bank}"])
                S.add("scalar", lambda E, tt=tt, oh=oh, bank=bank: E.copy(
                    ap(vaug, tt * 1056 + oh * 4 * 132,
                       [[132, 4], [1, 64]]),
                    ap(psB[bank], 0, [[128, 4], [1, 64]])),
                    reads=[f"B{bank}"], writes=["vaug"])
                S.add("scalar", lambda E, tt=tt, oh=oh, bank=bank: E.copy(
                    ap(vaug, tt * 1056 + oh * 4 * 132 + 66,
                       [[132, 4], [1, 64]]),
                    ap(psB[bank], 64, [[128, 4], [1, 64]])),
                    reads=[f"B{bank}"], writes=["vaug"])

    def wo_and_tail(s, tt):
        for oh in range(2):
            bank = 3 + ((2 * tt + oh) % 4)
            for dc in range(8):
                S.add("tensor", lambda E, tt=tt, oh=oh, dc=dc, bank=bank: E.matmul(
                    psB[bank], yT[:, dc * 512 + tt * 128:dc * 512 + (tt + 1) * 128],
                    wo_sb[:, dc * D + oh * 512:dc * D + (oh + 1) * 512],
                    start=(dc == 0), stop=(dc == 7)),
                    reads=["xnT", "wo"], writes=[f"B{bank}"])
            S.add("vector", lambda E, tt=tt, oh=oh, bank=bank: E.scalar_tensor_tensor(
                xseg[:, tt * D + oh * 512:tt * D + (oh + 1) * 512],
                psB[bank], 1.0,
                xseg[:, tt * D + oh * 512:tt * D + (oh + 1) * 512],
                OP.mult, OP.add, accum_out=qc(tt, oh)),
                reads=[f"B{bank}", f"xseg{tt}"],
                writes=[f"xseg{tt}", f"cQ_s{tt}"])
        S.add("sync", lambda E, s=s, tt=tt: E.dma_start(
            out=bass.AP(x1_d.tensor, (s * SEG + tt * 128) * D,
                        [[D, 128], [1, D]]),
            in_=xseg[:, tt * D:(tt + 1) * D]),
            reads=[f"xseg{tt}"], dma=True)
        xt = xseg[:, tt * D:(tt + 1) * D]
        S.add("vector", lambda E, tt=tt: E.tensor_tensor(
            qc(tt, 2), qc(tt, 0), qc(tt, 1), OP.add),
            reads=[f"cQ_s{tt}"], writes=[f"cQ_sum{tt}"])
        S.add("scalar", lambda E, xt=xt, tt=tt: E.activation(
            sqout, xt, AF.Square, accum_out=qc(tt, 3)),
            reads=[f"xseg{tt}"], writes=["sqout", f"cQ_sq{tt}"])
        S.add("vector", lambda E, xt=xt, tt=tt: E.tensor_reduce(
            qc(tt, 4), xt, AX.X, OP.max),
            reads=[f"xseg{tt}"], writes=[f"cQ_max{tt}"])
        S.add("vector", lambda E, xt=xt, tt=tt: E.tensor_reduce(
            qc(tt, 5), xt, AX.X, OP.min),
            reads=[f"xseg{tt}"], writes=[f"cQ_min{tt}"])
        S.add("vector", lambda E, tt=tt: E.tensor_scalar(
            qc(tt, 6), qc(tt, 2), 1.0 / D, None, OP.mult),
            reads=[f"cQ_sum{tt}"], writes=[f"cQ_m{tt}"])
        S.add("vector", lambda E, tt=tt: E.tensor_tensor(
            qc(tt, 7), qc(tt, 6), qc(tt, 6), OP.mult),
            reads=[f"cQ_m{tt}"], writes=[f"cQ_m2{tt}"])
        S.add("vector", lambda E, tt=tt: E.scalar_tensor_tensor(
            qc(tt, 8), qc(tt, 3), 1.0 / D, qc(tt, 7), OP.mult, OP.subtract),
            reads=[f"cQ_sq{tt}", f"cQ_m2{tt}"], writes=[f"cQ_var{tt}"])
        S.add("scalar", lambda E, tt=tt: E.activation(
            qc(tt, 9), qc(tt, 8), AF.Ln, bias=epscol),
            reads=[f"cQ_var{tt}", "epscol"], writes=[f"cQ_ln{tt}"])
        S.add("scalar", lambda E, tt=tt: E.activation(
            qc(tt, 10), qc(tt, 9), AF.Exp, scale=nhalfcol),
            reads=[f"cQ_ln{tt}", "nhalfcol"], writes=[f"cQ_rstd{tt}"])
        S.add("vector", lambda E, tt=tt: E.tensor_tensor(
            qc(tt, 11), qc(tt, 4), qc(tt, 6), OP.subtract),
            reads=[f"cQ_max{tt}", f"cQ_m{tt}"], writes=[f"cQ_a1{tt}"])
        S.add("vector", lambda E, tt=tt: E.tensor_tensor(
            qc(tt, 12), qc(tt, 6), qc(tt, 5), OP.subtract),
            reads=[f"cQ_m{tt}", f"cQ_min{tt}"], writes=[f"cQ_a2{tt}"])
        S.add("vector", lambda E, tt=tt: E.tensor_tensor(
            qc(tt, 11), qc(tt, 11), qc(tt, 12), OP.max),
            reads=[f"cQ_a1{tt}", f"cQ_a2{tt}"], writes=[f"cQ_a1{tt}"])
        S.add("vector", lambda E, tt=tt: E.scalar_tensor_tensor(
            qc(tt, 13), qc(tt, 11), QEPS, qc(tt, 10), OP.max, OP.mult),
            reads=[f"cQ_a1{tt}", f"cQ_rstd{tt}"], writes=[f"cQ_amx{tt}"])
        S.add("vector", lambda E, tt=tt: E.tensor_single_scalar(
            qc(tt, 13), qc(tt, 13), QEPS, OP.max),
            reads=[f"cQ_amx{tt}"], writes=[f"cQ_amx{tt}"])
        S.add("vector", lambda E, tt=tt: E.reciprocal(qc(tt, 14), qc(tt, 13)),
              reads=[f"cQ_amx{tt}"], writes=[f"cQ_ramx{tt}"])
        S.add("vector", lambda E, tt=tt: E.tensor_scalar(
            qc(tt, 15), qc(tt, 10), qc(tt, 14), 127.0, OP.mult, OP.mult),
            reads=[f"cQ_rstd{tt}", f"cQ_ramx{tt}"], writes=[f"qscl{tt}"])
        S.add("vector", lambda E, tt=tt: E.tensor_scalar(
            qc(tt, 7), qc(tt, 6), qc(tt, 15), -1.0, OP.mult, OP.mult),
            reads=[f"cQ_m{tt}", f"qscl{tt}"], writes=[f"qb2{tt}"])
        S.add("vector", lambda E, xt=xt, tt=tt: E.tensor_scalar(
            sqout, xt, qc(tt, 15), qc(tt, 7), OP.mult, OP.add),
            reads=[f"xseg{tt}", f"qscl{tt}", f"qb2{tt}"], writes=["sqout"])
        S.add("gpsimd", lambda E: E.tensor_scalar(
            xn_tm[0], sqout, M64, M64, OP.add, OP.subtract),
            reads=["sqout"], writes=["xntm0"])
        S.add("vector", lambda E: E.tensor_scalar(
            sqout, sqout, M32, M32, OP.add, OP.subtract),
            reads=["sqout"], writes=["sqout"])
        S.add("vector", lambda E: E.scalar_tensor_tensor(
            xn_tm[1], sqout, dprime, xn_tm[0], OP.mult, OP.add),
            reads=["sqout", "dprime", "xntm0"], writes=["xntm1"])
        S.add("vector", lambda E, tt=tt: E.tensor_scalar(
            xn_tm[1], xn_tm[1], qc(tt, 13), 1.0 / 127.0, OP.mult, OP.mult),
            reads=["xntm1", f"cQ_amx{tt}"], writes=["xntm1"])
        for g in range(2):
            for i in range(4):
                kc = g * 4 + i
                S.add("tensor", lambda E, kc=kc, i=i: E.transpose(
                    psT[:, i * 256:i * 256 + 128],
                    xn_tm[1][:, kc * 128:(kc + 1) * 128], ident),
                    reads=["xntm1", "ident"], writes=["psT"])
            S.add("scalar", lambda E, s=s, tt=tt, g=g: E.copy(
                ap(xq_sb, g * 4 * T + s * 512 + tt * 128,
                   [[T, 4], [1, 128]]),
                ap(psT, 0, [[256, 4], [1, 128]])),
                reads=["psT"], writes=["xq"])
        if s + 1 < NSEG:
            x_load(s + 1, tt)
            ln1_reduce(s + 1, tt)

    for tt in range(4):
        x_load(0, tt)
    for tt in range(4):
        ln1_reduce(0, tt)

    for s in range(NSEG):
        ln1_cols_xn(s)
        qkv_proj(s)
        for tt in range(4):
            for bg in range(3):
                ncols = 396 if bg < 2 else 264
                S.add("scalar", lambda E, bg=bg, ncols=ncols: E.copy(
                    sbf3[bg][:, 0:ncols], psB[bg][:, 0:ncols]),
                    reads=[f"B{bg}"], writes=[f"sbf{bg}"])
            def stage1(P):
                b = PAIR_BANK[P]
                sl = PAIR_SLOT[P]
                yb = 3 if P % 2 == 0 else 4
                scb = 5 + (P % 2)
                qsl = phiQ[:, P * 512 + tt * 128:P * 512 + (tt + 1) * 128]
                ksl = phiK[:, P * 512 + tt * 128:P * 512 + (tt + 1) * 128]
                S.add("tensor", lambda E, qsl=qsl, b=b, sl=sl, yb=yb: E.matmul(
                    psB[yb][:, 0:132], qsl, sbf3[b][:, sl * 132:(sl + 1) * 132],
                    start=True, stop=False, skip_group_check=True),
                    reads=["phiQ", f"sbf{b}"], writes=[f"B{yb}"])
                S.add("tensor", lambda E, qsl=qsl, ksl=ksl, scb=scb: E.matmul(
                    psB[scb][:, 0:128], ksl[0:64, :], qsl[0:64, :],
                    start=True, stop=False, skip_group_check=True),
                    reads=["phiK", "phiQ"], writes=[f"B{scb}"])
                S.add("tensor", lambda E, qsl=qsl, ksl=ksl, scb=scb: E.matmul(
                    psB[scb][:, 128:256], ksl[64:128, :], qsl[64:128, :],
                    start=False, stop=True, skip_group_check=True),
                    reads=["phiK", "phiQ"], writes=[f"B{scb}"])
                S.add("tensor", lambda E, ksl=ksl, P=P: E.transpose(
                    psT[:, (P % 2) * 256:(P % 2) * 256 + 128], ksl, ident),
                    reads=["phiK", "ident"], writes=["psT"])
                S.add("scalar", lambda E, P=P: E.copy(
                    phiKT[P % 2], psT[:, (P % 2) * 256:(P % 2) * 256 + 128]),
                    reads=["psT"], writes=[f"phiKT{P % 2}"])

            def stage2(P):
                b = PAIR_BANK[P]
                sl = PAIR_SLOT[P]
                yb = 3 if P % 2 == 0 else 4
                scb = 5 + (P % 2)
                pp = P % 2
                S.add("vector", lambda E, pp=pp, scb=scb: E.tensor_tensor(
                    attnT[pp], psB[scb][:, 0:256], mask2, OP.mult),
                    reads=[f"B{scb}", "mask2"], writes=[f"attnT{pp}"])
                for h in range(2):
                    vs = vaug[:, tt * 1056 + P * 132 + h * 66:
                              tt * 1056 + P * 132 + h * 66 + 65]
                    S.add("tensor", lambda E, pp=pp, h=h, yb=yb, vs=vs: E.matmul(
                        psB[yb][:, h * 66:h * 66 + 65],
                        attnT[pp][:, h * 128:(h + 1) * 128], vs,
                        start=False, stop=(h == 1), skip_group_check=True),
                        reads=[f"attnT{pp}", "vaug"], writes=[f"B{yb}"])
                for h in range(2):
                    vs = vaug[:, tt * 1056 + P * 132 + h * 66:
                              tt * 1056 + P * 132 + h * 66 + 65]
                    stt = first_S_mm[b]
                    first_S_mm[b] = False
                    S.add("tensor", lambda E, pp=pp, h=h, b=b, sl=sl, vs=vs, stt=stt: E.matmul(
                        psB[b][h * 64:h * 64 + 64, sl * 132 + h * 66:sl * 132 + h * 66 + 65],
                        phiKT[pp][:, h * 64:(h + 1) * 64], vs,
                        start=stt, stop=False, skip_group_check=True,
                        tile_position=(0, h * 64)),
                        reads=[f"phiKT{pp}", "vaug"], writes=[f"B{b}"])
                S.add("vector", lambda E, pp=pp, yb=yb: E.tensor_single_scalar(
                    rdc[:, pp * 4:pp * 4 + 2],
                    ap(psB[yb], 64, [[66, 2]]), ZEPS, OP.max),
                    reads=[f"B{yb}"], writes=[f"rdc{pp}"])
                S.add("vector", lambda E, pp=pp: E.reciprocal(
                    rdc[:, pp * 4 + 2:pp * 4 + 4], rdc[:, pp * 4:pp * 4 + 2]),
                    reads=[f"rdc{pp}"], writes=[f"rdc{pp}"])
                for h in range(2):
                    S.add("scalar", lambda E, pp=pp, yb=yb, h=h: E.activation(
                        y_tm[pp][:, h * 64:(h + 1) * 64],
                        psB[yb][:, h * 66:h * 66 + 64], AF.Copy,
                        scale=rdc[:, pp * 4 + 2 + h:pp * 4 + 3 + h]),
                        reads=[f"B{yb}", f"rdc{pp}"], writes=[f"ytm{pp}"])
                S.add("tensor", lambda E, pp=pp: E.transpose(
                    psT[:, 512 + pp * 256:512 + pp * 256 + 128], y_tm[pp], ident),
                    reads=[f"ytm{pp}", "ident"], writes=["psT"])
                S.add("scalar", lambda E, P=P, tt=tt, pp=pp: E.copy(
                    yT[:, P * 512 + tt * 128:P * 512 + (tt + 1) * 128],
                    psT[:, 512 + pp * 256:512 + pp * 256 + 128]),
                    reads=["psT"], writes=["xnT"])

            stage1(0)
            for P in range(1, 8):
                stage1(P)
                stage2(P - 1)
            stage2(7)
            emit_prepass(5)
            wo_and_tail(s, tt)
        emit_prepass(4)

    if t_xn is not None:
        S.add("sync", lambda E: E.dma_start(out=t_xn, in_=xnT),
              reads=["xnT"], dma=True)
    if t_phiq is not None:
        S.add("sync", lambda E: E.dma_start(out=t_phiq, in_=phiQ),
              reads=["phiQ"], dma=True)
    if t_phik is not None:
        S.add("sync", lambda E: E.dma_start(out=t_phik, in_=phiK),
              reads=["phiK"], dma=True)
    if t_vaug is not None:
        S.add("sync", lambda E: E.dma_start(out=bass.AP(
            t_vaug.tensor, 0, [[8 * 132, 128], [1, 4 * 8 * 132]]), in_=vaug),
            reads=["vaug"], dma=True)
    if t_y is not None:
        S.add("sync", lambda E: E.dma_start(out=t_y, in_=yT),
              reads=["xnT"], dma=True)
    if t_xq is not None:
        S.add("sync", lambda E: E.dma_start(out=t_xq, in_=xq_sb),
              reads=["xq"], dma=True)

    emit_prepass(192)
    emit_prepass_final()

    S.join()
    S.add("vector", lambda E: E.memset(maxacc, -3.0e38), writes=["maxacc"])
    S.add("vector", lambda E: E.memset(minacc, 3.0e38), writes=["minacc"])

    def wd_ternarize(jc):
        b = jc % 2
        S.add("gpsimd", lambda E, jc=jc, b=b: E.dma_start(
            out=wdbuf[b], in_=bass.AP(
                wd_d.tensor, jc * 128 * 1024, [[1024, 128], [1, 1024]])),
            writes=[f"wdbuf{b}"], dma=True)
        S.add("vector", lambda E, b=b: E.tensor_scalar(
            sqout, wdbuf[b], thr[:, 5:6], None, OP.is_le),
            reads=[f"wdbuf{b}", "thr"], writes=["sqout"])
        S.add("vector", lambda E, jc=jc, b=b: E.scalar_tensor_tensor(
            wdq_sb[:, jc * 1024:(jc + 1) * 1024], wdbuf[b], thr[:, 4:5],
            sqout, OP.is_ge, OP.subtract),
            reads=[f"wdbuf{b}", "thr", "sqout"], writes=["wdq"])

    for oc in range(32):
        wd_ternarize(oc)
        b = oc % 2
        S.add("sync", lambda E, oc=oc, b=b: E.dma_start(
            out=wstream[b][:, 0:1024], in_=bass.AP(
                wg_d.tensor, oc * 128 * 1024, [[1024, 128], [1, 1024]])),
            writes=[f"wstream{b}"], dma=True)
        S.add("sync", lambda E, oc=oc, b=b: E.dma_start(
            out=wstream[b][:, 1024:2048], in_=bass.AP(
                wu_d.tensor, oc * 128 * 1024, [[1024, 128], [1, 1024]])),
            writes=[f"wstream{b}"], dma=True)
        for (half, thri, wq2) in ((0, 0, wqg_sb), (1, 2, wqu_sb)):
            wsl = wstream[b][:, half * 1024:(half + 1) * 1024]
            S.add("vector", lambda E, wsl=wsl, thri=thri: E.tensor_scalar(
                sqout, wsl, thr[:, thri + 1:thri + 2], None, OP.is_le),
                reads=[f"wstream{b}", "thr"], writes=["sqout"])
            S.add("vector", lambda E, wsl=wsl, thri=thri, wq2=wq2, b=b: E.scalar_tensor_tensor(
                wq2[b], wsl, thr[:, thri:thri + 1], sqout, OP.is_ge, OP.subtract),
                reads=[f"wstream{b}", "thr", "sqout"], writes=[f"wq2_{id(wq2)}_{b}"])
        all_t8 = list(range(NSTRIP))
        t8_groups = [tuple(all_t8[i:i + 3]) for i in range(0, NSTRIP, 3)]
        for grp in t8_groups:
            ng = len(grp)
            gbanks = [0, 1, 2][:ng]
            ubanks = [3, 4, 5][:ng]
            for kc in range(8):
                for gi, t8 in enumerate(grp):
                    S.add("tensor", lambda E, kc=kc, b=b, gb=gbanks[gi], t8=t8: E.matmul(
                        psB[gb], wqg_sb[b][:, kc * 128:(kc + 1) * 128],
                        xq_sb[:, kc * T + t8 * 512:kc * T + (t8 + 1) * 512],
                        start=(kc == 0), stop=(kc == 7)),
                        reads=[f"wq2_{id(wqg_sb)}_{b}", "xq"], writes=[f"B{gbanks[gi]}"])
            for kc in range(8):
                for gi, t8 in enumerate(grp):
                    S.add("tensor", lambda E, kc=kc, b=b, ub=ubanks[gi], t8=t8: E.matmul(
                        psB[ub], wqu_sb[b][:, kc * 128:(kc + 1) * 128],
                        xq_sb[:, kc * T + t8 * 512:kc * T + (t8 + 1) * 512],
                        start=(kc == 0), stop=(kc == 7)),
                        reads=[f"wq2_{id(wqu_sb)}_{b}", "xq"], writes=[f"B{ubanks[gi]}"])
            for gi, t8 in enumerate(grp):
                gb, ub = gbanks[gi], ubanks[gi]
                pp = t8 % 2
                S.add("scalar", lambda E, gb=gb, pp=pp: E.activation(
                    g2_sb[pp], psB[gb], AF.Silu, scale=mw[:, 0:1]),
                    reads=[f"B{gb}", "mw"], writes=[f"g2_{pp}"])
                S.add("scalar", lambda E, ub=ub, pp=pp: E.copy(ud_sb[pp], psB[ub]),
                      reads=[f"B{ub}"], writes=[f"ud_{pp}"])
                S.add("vector", lambda E, pp=pp: E.tensor_tensor(
                    hbf_sb[pp], g2_sb[pp], ud_sb[pp], OP.mult),
                    reads=[f"g2_{pp}", f"ud_{pp}"], writes=[f"hbf{pp}"])
                S.add("vector", lambda E, pp=pp, t8=t8: E.tensor_tensor(
                    maxacc[:, t8 * 512:(t8 + 1) * 512],
                    maxacc[:, t8 * 512:(t8 + 1) * 512], hbf_sb[pp], OP.max),
                    reads=[f"hbf{pp}", "maxacc"], writes=["maxacc"])
                S.add("vector", lambda E, pp=pp, t8=t8: E.tensor_tensor(
                    minacc[:, t8 * 512:(t8 + 1) * 512],
                    minacc[:, t8 * 512:(t8 + 1) * 512], hbf_sb[pp], OP.min),
                    reads=[f"hbf{pp}", "minacc"], writes=["minacc"])
                for hh in range(2):
                    S.add("sync", lambda E, oc=oc, t8=t8, pp=pp, hh=hh: E.dma_start(
                        out=bass.AP(h_d.tensor,
                                    ((2 * t8 + hh) * INNER + oc * 128) * 256,
                                    [[256, 128], [1, 256]]),
                        in_=hbf_sb[pp][:, hh * 256:(hh + 1) * 256]),
                        reads=[f"hbf{pp}"], dma=True)
    S.add("vector", lambda E: E.tensor_single_scalar(minacc, minacc, -1.0, OP.mult),
          reads=["minacc"], writes=["minacc"])
    S.add("vector", lambda E: E.tensor_tensor(mxabs, maxacc, minacc, OP.max),
          reads=["maxacc", "minacc"], writes=["mxabs"])
    for c in range(NT128):
        sl = c % 4
        S.add("tensor", lambda E, c=c, sl=sl: E.transpose(
            psT[:, sl * 256:sl * 256 + 128], mxabs[:, c * 128:(c + 1) * 128], ident),
            reads=["mxabs", "ident"], writes=["psT"])
        S.add("vector", lambda E, c=c, sl=sl: E.tensor_reduce(
            mxw[:, c:c + 1], psT[:, sl * 256:sl * 256 + 128], AX.X, OP.max),
            reads=["psT"], writes=["mxw"])
    S.add("vector", lambda E: E.tensor_scalar(
        betac, mxw, mw[:, 1:2], QEPS, OP.mult, OP.max),
        reads=["mxw", "mw"], writes=["acl"])
    S.add("vector", lambda E: E.reciprocal(mxw, betac),
          reads=["acl"], writes=["rac"])
    S.add("vector", lambda E: E.tensor_scalar(
        mxw, mxw, mw[:, 1:2], 127.0, OP.mult, OP.mult),
        reads=["rac", "mw"], writes=["qscols"])
    S.add("vector", lambda E: E.tensor_scalar(
        betac, betac, mw[:, 2:3], 1.0 / 127.0, OP.mult, OP.mult),
        reads=["acl", "mw"], writes=["betac"])

    if t_wdq is not None:
        S.add("sync", lambda E: E.dma_start(out=t_wdq, in_=wdq_sb),
              reads=["wdq"], dma=True)
    if t_mx is not None:
        S.add("sync", lambda E: E.dma_start(out=t_mx, in_=mxw),
              reads=["qscols"], dma=True)

    S.join()

    def b2_strip_load(hs):
        buf = hs % 2
        for g in range(2):
            S.add("sync", lambda E, hs=hs, g=g, buf=buf: E.dma_start(
                out=ap(hstrip[buf], g * 16 * 256, [[256, 16], [1, 256]]),
                in_=bass.AP(h_d.tensor, (hs * INNER + g * 16 * 128) * 256,
                            [[256, 128], [128 * 256, 16], [1, 256]])),
                writes=[f"hstrip{buf}"], dma=True)

    def b2_quant(tc):
        c = tc % 2
        hs = tc // 2
        buf = hs % 2
        hq = hq_sb[tc % 2]
        S.add("tensor", lambda E, tc=tc: E.transpose(
            bass.AP(psB[6].tensor, psB[6].offset, [[512, 1], [1, 128]]),
            mxw[:, tc:tc + 1], identf),
            reads=["qscols", "identf"], writes=["B6"])
        S.add("scalar", lambda E, tc=tc: E.copy(
            qrow1[tc % 2], bass.AP(psB[6].tensor, psB[6].offset, [[512, 1], [1, 128]])),
            reads=["B6"], writes=[f"qrow1{tc % 2}"])
        S.add("tensor", lambda E, tc=tc: E.matmul(
            psB[6][:, 0:128], onerow_f, qrow1[tc % 2], start=True, stop=True),
            reads=["onerow", f"qrow1{tc % 2}"], writes=["B6"])
        S.add("scalar", lambda E, tc=tc: E.copy(qsb[tc % 2], psB[6][:, 0:128]),
              reads=["B6"], writes=[f"qsb{tc % 2}"])
        for qq in range(4):
            hsl = ap(hstrip[buf], qq * 8 * 256 + c * 128, [[256, 8], [1, 128]])
            S.add("gpsimd", lambda E, hsl=hsl, tc=tc: E.tensor_tensor(
                ap(qy, 0, [[128, 8], [1, 128]]), hsl,
                ap(qsb[tc % 2], 0, [[0, 8], [1, 128]]), OP.mult),
                reads=[f"hstrip{buf}", f"qsb{tc % 2}"], writes=["qy"])
            S.add("vector", lambda E: E.tensor_scalar(
                qC, qy, M64, M64, OP.add, OP.subtract),
                reads=["qy"], writes=["qC"])
            S.add("vector", lambda E: E.tensor_scalar(
                qy, qy, M32, M32, OP.add, OP.subtract),
                reads=["qy"], writes=["qy"])
            S.add("vector", lambda E, hq=hq, qq=qq, tc=tc: E.scalar_tensor_tensor(
                hq[:, qq * 1024:(qq + 1) * 1024], qy, dprime, qC,
                OP.mult, OP.add),
                reads=["qy", "dprime", "qC"], writes=[f"hq{tc % 2}"])
        S.add("sync", lambda E, tc=tc: E.dma_start(
            out=x1ld[tc % 2], in_=bass.AP(x1_d.tensor, tc * 128 * D,
                                          [[D, 128], [1, D]])),
            writes=[f"x1ld{tc % 2}"], dma=True)

    def b2_mm(tc):
        hq = hq_sb[tc % 2]
        bA, bB = (3, 4) if tc % 2 == 0 else (5, 2)
        for jo in range(32):
            for oh, bank in ((0, bA), (1, bB)):
                S.add("tensor", lambda E, jo=jo, oh=oh, bank=bank, hq=hq, tc=tc: E.matmul(
                    psB[bank], hq[:, jo * 128:(jo + 1) * 128],
                    wdq_sb[:, jo * D + oh * 512:jo * D + (oh + 1) * 512],
                    start=(jo == 0), stop=(jo == 31)),
                    reads=[f"hq{tc % 2}", "wdq"], writes=[f"B{bank}"])
        for oh, bank in ((0, bA), (1, bB)):
            S.add("vector", lambda E, oh=oh, bank=bank, tc=tc: E.scalar_tensor_tensor(
                outt[tc % 2][:, oh * 512:(oh + 1) * 512], psB[bank],
                betac[:, tc:tc + 1], x1ld[tc % 2][:, oh * 512:(oh + 1) * 512],
                OP.mult, OP.add),
                reads=[f"B{bank}", "betac", f"x1ld{tc % 2}"],
                writes=[f"outt{tc % 2}"])
        S.add("sync", lambda E, tc=tc: E.dma_start(
            out=bass.AP(out_d.tensor, tc * 128 * D, [[D, 128], [1, D]]),
            in_=outt[tc % 2]),
            reads=[f"outt{tc % 2}"], dma=True)

    b2_strip_load(0)
    b2_strip_load(1)
    for tc in range(NT128):
        b2_quant(tc)
        if tc % 2 == 1 and tc // 2 + 2 < NHSTRIP:
            b2_strip_load(tc // 2 + 2)
        if tc >= 1:
            b2_mm(tc - 1)
    b2_mm(NT128 - 1)

    fin_waits = [(c, S.ctr[c]) for c in DMA_LANES if S.ctr[c] > 0]
    S.add("vector", lambda E: E.memset(redcol, 0.0),
          waits=fin_waits, writes=["redcol"])

    S.emit(nc, sems)
    ctx.close()
    return nc, taps


def _host_layouts(W_q, W_k, W_v, W_o, w_gate, w_up, w_down):
    def attw(W):
        return np.ascontiguousarray(
            W.T.reshape(8, 128, 1024).transpose(1, 0, 2).reshape(128, 8192))

    def ffnw(W):
        g = W.T.reshape(8, 128, 32, 128)
        return np.ascontiguousarray(g.transpose(2, 1, 0, 3).reshape(32, 128, 1024))

    def dw(W):
        return np.ascontiguousarray(W.T.reshape(32, 128, 1024))

    return (attw(W_q), attw(W_k), attw(W_v), attw(W_o),
            ffnw(w_gate), ffnw(w_up), dw(w_down))


_NC_CACHE = {}


def kernel(L0_attn_x, L0_attn_W_q, L0_attn_W_k, L0_attn_W_v, L0_attn_W_o,
           L0_ffn_w_gate, L0_ffn_w_up, L0_ffn_w_down):
    x = np.asarray(L0_attn_x, dtype=np.float32)
    B, T, _ = x.shape
    wq, wk, wv, wo, wg, wu, wd = _host_layouts(
        np.asarray(L0_attn_W_q, np.float32), np.asarray(L0_attn_W_k, np.float32),
        np.asarray(L0_attn_W_v, np.float32), np.asarray(L0_attn_W_o, np.float32),
        np.asarray(L0_ffn_w_gate, np.float32), np.asarray(L0_ffn_w_up, np.float32),
        np.asarray(L0_ffn_w_down, np.float32))
    key = T
    if key not in _NC_CACHE:
        _NC_CACHE[key] = build_program(T=T)[0]
    nc = _NC_CACHE[key]
    in_maps = []
    for c in range(B):
        in_maps.append({
            "x": np.ascontiguousarray(x[c]),
            "wqT": wq, "wkT": wk, "wvT": wv, "woT": wo,
            "wgT": wg, "wuT": wu, "wdT": wd,
        })
    res = run_bass_kernel_spmd(nc, in_maps, core_ids=list(range(B)))
    return np.stack([res.results[c]["out"] for c in range(B)], axis=0)


# version 14
# speedup vs baseline: 1.2513x; 1.2513x over previous
import numpy as np
import ml_dtypes
import concourse.bass as bass
import concourse.mybir as mybir
import bass_rust as _br
from concourse.bass_utils import run_bass_kernel_spmd
from contextlib import ExitStack

F32 = mybir.dt.float32
BF16 = mybir.dt.bfloat16
OP = mybir.AluOpType
AF = mybir.ActivationFunctionType
AX = mybir.AxisListType

M32 = 12582912.0
M64 = 6755399441055744.0

D = 1024
INNER = 4096
SEG = 512
LNEPS = 1e-5
ZEPS = 1e-6
QEPS = 1e-5

ENGINES = ("sync", "tensor", "vector", "scalar", "gpsimd")
ENG_CTR = {"sync": "sS", "tensor": "tS", "vector": "vS", "scalar": "cS",
           "gpsimd": "gS"}
DMA_CTR = {"sync": "qsx", "gpsimd": "qgp", "scalar": "qsc"}
NLANES = 6
DMA_LANES = [f"{q}{l}" for q in ("qsx", "qgp", "qsc") for l in range(NLANES)]
ALL_CTRS = tuple(DMA_LANES) + ("sS", "tS", "vS", "cS", "gS")


class Sched:
    def __init__(self):
        self.ops = {e: [] for e in ENGINES}
        self.ctr = {c: 0 for c in ALL_CTRS}
        self.dma_i = {"qsx": 0, "qgp": 0, "qsc": 0}
        self.res = {}
        self.mark = {e: {} for e in ENGINES}
        self.pending_join = {e: None for e in ENGINES}

    def _r(self, name):
        if name not in self.res:
            self.res[name] = {"w": None, "r": []}
        return self.res[name]

    def add(self, eng, fn, reads=(), writes=(), waits=(), dma=False):
        if dma:
            q = DMA_CTR[eng]
            i = self.dma_i[q]
            self.dma_i[q] = i + 1
            cname = f"{q}{i % NLANES}"
            step = 16
            lane_reuse = (cname, 16 * (i // NLANES)) if i >= NLANES else None
        else:
            cname = ENG_CTR[eng]
            step = 1
            lane_reuse = None
        wl = {}

        def need(dep):
            if dep is None:
                return
            c, v = dep
            if v <= 0 or self.mark[eng].get(c, 0) >= v:
                return
            wl[c] = max(wl.get(c, 0), v)

        if lane_reuse is not None:
            need(lane_reuse)
        jw = self.pending_join[eng]
        if jw:
            for c, v in jw.items():
                need((c, v))
            self.pending_join[eng] = None
        for c, v in waits:
            need((c, v))
        for rn in reads:
            need(self._r(rn)["w"])
        for wn in writes:
            r = self._r(wn)
            need(r["w"])
            for dep in r["r"]:
                need(dep)
        self.ctr[cname] += step
        me = (cname, self.ctr[cname])
        for rn in reads:
            self._r(rn)["r"].append(me)
        for wn in writes:
            r = self._r(wn)
            r["w"] = me
            r["r"] = []
        for c, v in wl.items():
            assert v <= self.ctr[c], f"future wait {c}={v} now={self.ctr[c]}"
            self.mark[eng][c] = max(self.mark[eng].get(c, 0), v)
        self.ops[eng].append((tuple(sorted(wl.items())), fn, cname, step))
        return me

    def join(self):
        snap = dict(self.ctr)
        for e in ENGINES:
            self.pending_join[e] = dict(snap)
            self.mark[e] = {}

    def emit(self, nc, sems):
        with nc.Block() as blk:
            for eng in ENGINES:
                ops = self.ops[eng]

                def body(E, ops=ops):
                    for waits, fn, cname, step in ops:
                        for (c, v) in waits:
                            E.wait_ge(sems[c], v)
                        inst = fn(E)
                        inst.then_inc(sems[cname], step)
                getattr(blk, eng)(body)


def ap(t, off, dims):
    p = [list(d) for d in t.ap][0]
    return bass.AP(t.tensor, t.offset + off, [p] + [list(d) for d in dims])


def build_program(T=4096, debug_taps=()):
    NSEG = T // SEG
    NT128 = T // 128
    NSTRIP = T // 512
    NHSTRIP = T // 256

    nc = bass.Bass("TRN2", target_bir_lowering=False, debug=False)

    x_d = nc.dram_tensor("x", [T, D], F32, kind="ExternalInput").ap()
    wq_d = nc.dram_tensor("wqT", [128, 8 * D], F32, kind="ExternalInput").ap()
    wk_d = nc.dram_tensor("wkT", [128, 8 * D], F32, kind="ExternalInput").ap()
    wv_d = nc.dram_tensor("wvT", [128, 8 * D], F32, kind="ExternalInput").ap()
    wo_d = nc.dram_tensor("woT", [128, 8 * D], F32, kind="ExternalInput").ap()
    wg_d = nc.dram_tensor("wgT", [32, 128, 8 * 128], F32, kind="ExternalInput").ap()
    wu_d = nc.dram_tensor("wuT", [32, 128, 8 * 128], F32, kind="ExternalInput").ap()
    wd_d = nc.dram_tensor("wdT", [32, 128, D], F32, kind="ExternalInput").ap()
    out_d = nc.dram_tensor("out", [T, D], F32, kind="ExternalOutput").ap()
    x1_d = nc.dram_tensor("x1_scr", [T, D], F32, kind="Internal").ap()
    h_d = nc.dram_tensor("h_scr", [NHSTRIP, INNER, 256], BF16, kind="Internal").ap()

    taps = {}

    def tap(name, shape, dtype=F32):
        if name in debug_taps:
            taps[name] = nc.dram_tensor("tap_" + name, shape, dtype,
                                        kind="ExternalOutput").ap()
        return taps.get(name)

    t_xn = tap("xn", [128, 8 * SEG], BF16)
    t_phiq = tap("phiq", [128, 8 * SEG], BF16)
    t_phik = tap("phik", [128, 8 * SEG], BF16)
    t_vaug = tap("vaug", [4, 128, 8 * 132], BF16)
    t_y = tap("y", [128, 8 * SEG], BF16)
    t_xq = tap("xq", [128, 8 * T], BF16)
    t_wdq = tap("wdq", [128, 32 * D], BF16)
    t_mx = tap("mx", [128, NT128])

    ctx = ExitStack()
    sb = lambda name, shape, dt=F32: ctx.enter_context(
        nc.sbuf_tensor(name, shape, dt)).ap()
    psb = lambda name, shape, dt=F32: ctx.enter_context(
        nc.psum_tensor(name, shape, dt)).ap()

    iot = sb("iot", [128, 128])
    ident = sb("ident", [128, 128], BF16)
    identf = sb("identf", [128, 128])
    mask2 = sb("mask2", [128, 256], BF16)
    onecol = sb("onecol", [128, 1])
    epscol = sb("epscol", [128, 1])
    nhalfcol = sb("nhalfcol", [128, 1])
    onerow_f = sb("onerow_f", [1, 128])
    onesmat = sb("onesmat", [128, 128])
    disc = sb("disc", [128, 1])
    dprime = sb("dprime", [128, 1])
    mw = sb("mw", [128, 4])
    thr = sb("thr", [128, 6])
    wsum = sb("wsum", [128, 4])
    redcol = sb("redcol", [128, 1])
    bigacc = sb("bigacc", [128, 384])
    colsA = sb("colsA", [128, 32])
    colsQ = sb("colsQ", [128, 64])
    rdc = sb("rdc", [128, 8])
    mxw = sb("mxw", [128, NT128])
    qrow1 = [sb(f"qrow1_{i}", [1, 128]) for i in range(2)]
    betac = sb("betac", [128, NT128])

    A1 = sb("arena1", [128, 4 * 8 * D], BF16)
    A2 = sb("arena2", [128, 22528], BF16)
    A3 = sb("arena3", [128, 7424])
    XQ = sb("xq_arena", [128, max(8 * T, 24576)], BF16)

    wq_sb = A1[:, 0:8192]
    wk_sb = A1[:, 8192:16384]
    wv_sb = A1[:, 16384:24576]
    wo_sb = A1[:, 24576:32768]
    wdq_sb = A1[:, 0:32768]

    xnT = A2[:, 0:4096]
    phiQ = A2[:, 4096:8192]
    phiK = A2[:, 8192:12288]
    yT = A2[:, 0:4096]
    vaug = A2[:, 12288:16512]
    xn_tm = [A2[:, 16512 + i * 1024:16512 + (i + 1) * 1024] for i in range(2)]
    attnT = [A2[:, 18560 + i * 256:18560 + (i + 1) * 256] for i in range(2)]
    phiKT = [A2[:, 19072 + i * 128:19072 + (i + 1) * 128] for i in range(2)]
    sbf3 = [A2[:, 19328 + i * 396:19328 + (i + 1) * 396] for i in range(3)]
    y_tm = [A2[:, 20516 + i * 128:20516 + (i + 1) * 128] for i in range(2)]
    maxacc = A2[:, 0:T]
    minacc = A2[:, T:2 * T]
    wqg_sb = [A2[:, 8192 + i * 1024:8192 + (i + 1) * 1024] for i in range(2)]
    wqu_sb = [A2[:, 10240 + i * 1024:10240 + (i + 1) * 1024] for i in range(2)]
    g2_sb = [A2[:, 12288 + i * 512:12288 + (i + 1) * 512] for i in range(2)]
    ud_sb = [A2[:, 13312 + i * 512:13312 + (i + 1) * 512] for i in range(2)]
    hbf_sb = [A2[:, 14336 + i * 512:14336 + (i + 1) * 512] for i in range(2)]
    mxabs = A2[:, 18432:18432 + T]

    xseg = A3[:, 0:4096]
    ptmp = [A3[:, 4096 + i * 512:4096 + (i + 1) * 512] for i in range(2)]
    meanbuf = [A3[:, 5376 + i * 256:5376 + (i + 1) * 256] for i in range(4)]
    sqout = A3[:, 6400:7424]
    wstream = [A3[:, i * 2048:(i + 1) * 2048] for i in range(2)]
    wdbuf = [A3[:, 4096 + i * 1024:4096 + (i + 1) * 1024] for i in range(2)]
    meanbig = [A3[:, 5376 + i * 512:5376 + (i + 1) * 512] for i in range(2)]
    qy = A3[:, 0:1024]
    qC = A3[:, 1024:2048]
    x1ld = [A3[:, 2048 + i * 1024:2048 + (i + 1) * 1024] for i in range(2)]
    outt = [A3[:, 4096 + i * 1024:4096 + (i + 1) * 1024] for i in range(2)]
    qsb = [A3[:, 6144 + i * 128:6144 + (i + 1) * 128] for i in range(2)]

    xq_sb = XQ[:, 0:8 * T]
    hstrip = [XQ[:, i * 8192:(i + 1) * 8192] for i in range(2)]
    hq_sb = [XQ[:, 16384 + i * 4096:16384 + (i + 1) * 4096] for i in range(2)]

    psB = [psb(f"psB{i}", [128, 512]) for i in range(7)]
    psT = psb("psT", [128, 1024], BF16)

    sems = {}
    for n in ALL_CTRS:
        sems[n] = ctx.enter_context(nc.semaphore("sem_" + n))

    S = Sched()

    S.add("gpsimd", lambda E: E.iota(iot, [[1, 128]], channel_multiplier=-1,
                                     allow_small_or_imprecise_dtypes=True),
          writes=["iot"])
    S.add("vector", lambda E: E.tensor_single_scalar(ident, iot, 0.0, OP.is_equal),
          reads=["iot"], writes=["ident"])
    S.add("vector", lambda E: E.tensor_single_scalar(identf, iot, 0.0, OP.is_equal),
          reads=["iot"], writes=["identf"])
    S.add("vector", lambda E: E.tensor_single_scalar(
        ap(mask2, 0, [[128, 2], [1, 128]]),
        ap(iot, 0, [[0, 2], [1, 128]]), 0.0, OP.is_ge),
        reads=["iot"], writes=["mask2"])
    S.add("vector", lambda E: E.memset(onecol, 1.0), writes=["onecol"])
    S.add("vector", lambda E: E.memset(epscol, LNEPS), writes=["epscol"])
    S.add("vector", lambda E: E.memset(nhalfcol, -0.5), writes=["nhalfcol"])
    S.add("vector", lambda E: E.memset(onerow_f, 1.0), writes=["onerow"])
    S.add("vector", lambda E: E.memset(onesmat, 1.0), writes=["onesmat"])
    S.add("vector", lambda E: E.tensor_scalar(disc, onecol, M64, M64,
                                              OP.add, OP.subtract),
          reads=["onecol"], writes=["disc"])
    S.add("vector", lambda E: E.tensor_scalar(dprime, disc, -1.0, 1.0,
                                              OP.mult, OP.add),
          reads=["disc"], writes=["dprime"])
    for b in range(3):
        S.add("vector", lambda E, b=b: E.memset(psB[b], 0.0), writes=[f"B{b}"])
    S.add("vector", lambda E: E.memset(vaug, 0.0), writes=["vaug"])
    S.add("vector", lambda E: E.memset(
        ap(vaug, 64, [[1056, 4], [132, 8], [66, 2]]), 1.0),
        writes=["vaug"])

    for (dst, src, nm) in ((wq_sb, wq_d, "wq"), (wk_sb, wk_d, "wk"),
                           (wv_sb, wv_d, "wv"), (wo_sb, wo_d, "wo")):
        S.add("gpsimd", lambda E, d=dst, s=src: E.dma_start(out=d, in_=s),
              writes=[nm], dma=True)

    S.add("vector", lambda E: E.memset(bigacc, 0.0), writes=["bigacc"])
    mean_srcs = []
    for oc in range(32):
        mean_srcs.append((wg_d, oc * 128 * 1024))
    for oc in range(32):
        mean_srcs.append((wu_d, oc * 128 * 1024))
    for jc in range(32):
        mean_srcs.append((wd_d, jc * 128 * 1024))
    mean_srcs = mean_srcs[0:32][::1] + mean_srcs[32:64] + mean_srcs[64:96]
    _prep = {"i": 0}

    def emit_prepass(n):
        while n > 0 and _prep["i"] < 192:
            i = _prep["i"]
            _prep["i"] += 1
            n -= 1
            srct, off = mean_srcs[i // 2]
            half = i % 2
            b = i % 2
            sub = bass.AP(srct.tensor, off + half * 512, [[1024, 128], [1, 512]])
            S.add("sync", lambda E, b=b, s=sub: E.dma_start(out=meanbig[b], in_=s),
                  writes=[f"meanbig{b}"], dma=True)
            S.add("scalar", lambda E, b=b, i=i: E.activation(
                meanbig[b], meanbig[b], AF.Abs, accum_out=bigacc[:, i:i + 1]),
                reads=[f"meanbig{b}", "bigacc"], writes=[f"meanbig{b}", f"bigacc{i}"])

    def emit_prepass_final():
        for m in range(3):
            S.add("tensor", lambda E, m=m: E.matmul(
                psB[6][:, 0:64], onesmat, bigacc[:, m * 64:(m + 1) * 64],
                start=True, stop=True),
                reads=["onesmat"] + [f"bigacc{i}" for i in range(m * 64, (m + 1) * 64)],
                writes=["B6"])
            S.add("vector", lambda E, m=m: E.tensor_reduce(
                mw[:, m:m + 1], psB[6][:, 0:64], AX.X, OP.add),
                reads=["B6"], writes=["mwraw"])
        S.add("vector", lambda E: E.tensor_scalar(mw[:, 0:3], mw[:, 0:3],
                                                  1.0 / (INNER * D), QEPS,
                                                  OP.mult, OP.max),
              reads=["mwraw"], writes=["mw"])
        S.add("vector", lambda E: E.tensor_scalar(
            ap(thr, 0, [[2, 3], [1, 1]]), mw[:, 0:3], 0.5, None, OP.mult),
            reads=["mw"], writes=["thr"])
        S.add("vector", lambda E: E.tensor_scalar(
            ap(thr, 1, [[2, 3], [1, 1]]), mw[:, 0:3], -0.5, None, OP.mult),
            reads=["mw"], writes=["thr"])

    PAIR_BANK = {0: 0, 1: 0, 2: 0, 3: 1, 4: 1, 5: 1, 6: 2, 7: 2}
    PAIR_SLOT = {0: 0, 1: 1, 2: 2, 3: 0, 4: 1, 5: 2, 6: 0, 7: 1}
    first_S_mm = [True, True, True]

    def qc(tt, k):
        return colsQ[:, tt * 16 + k:tt * 16 + k + 1]

    def x_load(s, tt):
        S.add("sync", lambda E, s=s, tt=tt: E.dma_start(
            out=xseg[:, tt * D:(tt + 1) * D],
            in_=bass.AP(x_d.tensor, (s * SEG + tt * 128) * D,
                        [[D, 128], [1, D]])),
            writes=[f"xseg{tt}"], dma=True)

    def ln1_reduce(s, tt):
        xt = xseg[:, tt * D:(tt + 1) * D]
        S.add("scalar", lambda E, xt=xt, tt=tt: E.activation(
            sqout, xt, AF.Square, accum_out=colsA[:, 4 + tt:5 + tt]),
            reads=[f"xseg{tt}"], writes=["sqout", "cA_sq"])
        S.add("vector", lambda E, xt=xt, tt=tt: E.tensor_reduce(
            colsA[:, tt:tt + 1], xt, AX.X, OP.add),
            reads=[f"xseg{tt}"], writes=["cA_sum"])

    def ln1_cols_xn(s):
        S.add("vector", lambda E: E.tensor_scalar(
            colsA[:, 8:12], colsA[:, 0:4], 1.0 / D, None, OP.mult),
            reads=["cA_sum"], writes=["cA_m"])
        S.add("vector", lambda E: E.tensor_tensor(
            colsA[:, 12:16], colsA[:, 8:12], colsA[:, 8:12], OP.mult),
            reads=["cA_m"], writes=["cA_m2"])
        S.add("vector", lambda E: E.scalar_tensor_tensor(
            colsA[:, 16:20], colsA[:, 4:8], 1.0 / D, colsA[:, 12:16],
            OP.mult, OP.subtract),
            reads=["cA_sq", "cA_m2"], writes=["cA_var"])
        S.add("scalar", lambda E: E.activation(
            colsA[:, 20:24], colsA[:, 16:20], AF.Ln, bias=epscol),
            reads=["cA_var", "epscol"], writes=["cA_ln"])
        S.add("scalar", lambda E: E.activation(
            colsA[:, 24:28], colsA[:, 20:24], AF.Exp, scale=nhalfcol),
            reads=["cA_ln", "nhalfcol"], writes=["cA_rstd"])
        S.add("vector", lambda E: E.scalar_tensor_tensor(
            colsA[:, 28:32], colsA[:, 8:12], -1.0, colsA[:, 24:28],
            OP.mult, OP.mult),
            reads=["cA_m", "cA_rstd"], writes=["cA_mb"])
        for tt in range(4):
            xt = xseg[:, tt * D:(tt + 1) * D]
            S.add("vector", lambda E, xt=xt, tt=tt: E.tensor_scalar(
                xn_tm[tt % 2], xt, colsA[:, 24 + tt:25 + tt],
                colsA[:, 28 + tt:29 + tt], OP.mult, OP.add),
                reads=[f"xseg{tt}", "cA_rstd", "cA_mb"], writes=[f"xntm{tt % 2}"])
            for g in range(2):
                for i in range(4):
                    ic = g * 4 + i
                    S.add("tensor", lambda E, tt=tt, ic=ic, i=i: E.transpose(
                        psT[:, i * 256:i * 256 + 128],
                        xn_tm[tt % 2][:, ic * 128:(ic + 1) * 128], ident),
                        reads=[f"xntm{tt % 2}", "ident"], writes=["psT"])
                S.add("scalar", lambda E, tt=tt, g=g: E.copy(
                    ap(xnT, g * 4 * 512 + tt * 128, [[512, 4], [1, 128]]),
                    ap(psT, 0, [[256, 4], [1, 128]])),
                    reads=["psT"], writes=["xnT"])

    def qkv_proj(s):
        for (which, wsb, wres, phidst, phin) in (
                ("q", wq_sb, "wq", phiQ, "phiQ"), ("k", wk_sb, "wk", phiK, "phiK")):
            for og in range(4):
                b0 = 3 + (og % 2) * 2
                for kc in range(8):
                    for ob in range(2):
                        oc = og * 2 + ob
                        bank = b0 + ob
                        S.add("tensor", lambda E, oc=oc, kc=kc, bank=bank, wsb=wsb: E.matmul(
                            psB[bank], wsb[:, kc * D + oc * 128:kc * D + oc * 128 + 128],
                            xnT[:, kc * 512:(kc + 1) * 512],
                            start=(kc == 0), stop=(kc == 7)),
                            reads=[wres, "xnT"], writes=[f"B{bank}"])
                for ob in range(2):
                    oc = og * 2 + ob
                    bank = b0 + ob
                    S.add("vector", lambda E, oc=oc, bank=bank: E.tensor_single_scalar(
                        ptmp[oc % 2], psB[bank], 0.0, OP.min),
                        reads=[f"B{bank}"], writes=[f"ptmp{oc % 2}"])
                    S.add("scalar", lambda E, oc=oc: E.activation(
                        ptmp[oc % 2], ptmp[oc % 2], AF.Exp),
                        reads=[f"ptmp{oc % 2}"], writes=[f"ptmp{oc % 2}"])
                    S.add("vector", lambda E, oc=oc, bank=bank, phidst=phidst, phin=phin: E.scalar_tensor_tensor(
                        phidst[:, oc * 512:(oc + 1) * 512], psB[bank], 0.0,
                        ptmp[oc % 2], OP.max, OP.add),
                        reads=[f"B{bank}", f"ptmp{oc % 2}"], writes=[phin])
        for tt in range(4):
            b0 = 3 + (tt % 2) * 2
            for ic in range(8):
                for oh in range(2):
                    bank = b0 + oh
                    S.add("tensor", lambda E, tt=tt, oh=oh, ic=ic, bank=bank: E.matmul(
                        psB[bank], xnT[:, ic * 512 + tt * 128:ic * 512 + (tt + 1) * 128],
                        wv_sb[:, ic * D + oh * 512:ic * D + (oh + 1) * 512],
                        start=(ic == 0), stop=(ic == 7)),
                        reads=["xnT", "wv"], writes=[f"B{bank}"])
            for oh in range(2):
                bank = b0 + oh
                S.add("scalar", lambda E, tt=tt, oh=oh, bank=bank: E.copy(
                    ap(vaug, tt * 1056 + oh * 4 * 132,
                       [[132, 4], [1, 64]]),
                    ap(psB[bank], 0, [[128, 4], [1, 64]])),
                    reads=[f"B{bank}"], writes=["vaug"])
                S.add("scalar", lambda E, tt=tt, oh=oh, bank=bank: E.copy(
                    ap(vaug, tt * 1056 + oh * 4 * 132 + 66,
                       [[132, 4], [1, 64]]),
                    ap(psB[bank], 64, [[128, 4], [1, 64]])),
                    reads=[f"B{bank}"], writes=["vaug"])

    def wo_and_tail(s, tt):
        b0w = 3 + (tt % 2) * 2
        for dc in range(8):
            for oh in range(2):
                bank = b0w + oh
                S.add("tensor", lambda E, tt=tt, oh=oh, dc=dc, bank=bank: E.matmul(
                    psB[bank], yT[:, dc * 512 + tt * 128:dc * 512 + (tt + 1) * 128],
                    wo_sb[:, dc * D + oh * 512:dc * D + (oh + 1) * 512],
                    start=(dc == 0), stop=(dc == 7)),
                    reads=["xnT", "wo"], writes=[f"B{bank}"])
        for oh in range(2):
            bank = b0w + oh
            S.add("vector", lambda E, tt=tt, oh=oh, bank=bank: E.scalar_tensor_tensor(
                xseg[:, tt * D + oh * 512:tt * D + (oh + 1) * 512],
                psB[bank], 1.0,
                xseg[:, tt * D + oh * 512:tt * D + (oh + 1) * 512],
                OP.mult, OP.add, accum_out=qc(tt, oh)),
                reads=[f"B{bank}", f"xseg{tt}"],
                writes=[f"xseg{tt}", f"cQ_s{tt}"])
        S.add("sync", lambda E, s=s, tt=tt: E.dma_start(
            out=bass.AP(x1_d.tensor, (s * SEG + tt * 128) * D,
                        [[D, 128], [1, D]]),
            in_=xseg[:, tt * D:(tt + 1) * D]),
            reads=[f"xseg{tt}"], dma=True)
        xt = xseg[:, tt * D:(tt + 1) * D]
        S.add("vector", lambda E, tt=tt: E.tensor_tensor(
            qc(tt, 2), qc(tt, 0), qc(tt, 1), OP.add),
            reads=[f"cQ_s{tt}"], writes=[f"cQ_sum{tt}"])
        S.add("scalar", lambda E, xt=xt, tt=tt: E.activation(
            sqout, xt, AF.Square, accum_out=qc(tt, 3)),
            reads=[f"xseg{tt}"], writes=["sqout", f"cQ_sq{tt}"])
        S.add("vector", lambda E, xt=xt, tt=tt: E.tensor_reduce(
            qc(tt, 4), xt, AX.X, OP.max),
            reads=[f"xseg{tt}"], writes=[f"cQ_max{tt}"])
        S.add("vector", lambda E, xt=xt, tt=tt: E.tensor_reduce(
            qc(tt, 5), xt, AX.X, OP.min),
            reads=[f"xseg{tt}"], writes=[f"cQ_min{tt}"])
        S.add("vector", lambda E, tt=tt: E.tensor_scalar(
            qc(tt, 6), qc(tt, 2), 1.0 / D, None, OP.mult),
            reads=[f"cQ_sum{tt}"], writes=[f"cQ_m{tt}"])
        S.add("vector", lambda E, tt=tt: E.tensor_tensor(
            qc(tt, 7), qc(tt, 6), qc(tt, 6), OP.mult),
            reads=[f"cQ_m{tt}"], writes=[f"cQ_m2{tt}"])
        S.add("vector", lambda E, tt=tt: E.scalar_tensor_tensor(
            qc(tt, 8), qc(tt, 3), 1.0 / D, qc(tt, 7), OP.mult, OP.subtract),
            reads=[f"cQ_sq{tt}", f"cQ_m2{tt}"], writes=[f"cQ_var{tt}"])
        S.add("scalar", lambda E, tt=tt: E.activation(
            qc(tt, 9), qc(tt, 8), AF.Ln, bias=epscol),
            reads=[f"cQ_var{tt}", "epscol"], writes=[f"cQ_ln{tt}"])
        S.add("scalar", lambda E, tt=tt: E.activation(
            qc(tt, 10), qc(tt, 9), AF.Exp, scale=nhalfcol),
            reads=[f"cQ_ln{tt}", "nhalfcol"], writes=[f"cQ_rstd{tt}"])
        S.add("vector", lambda E, tt=tt: E.tensor_tensor(
            qc(tt, 11), qc(tt, 4), qc(tt, 6), OP.subtract),
            reads=[f"cQ_max{tt}", f"cQ_m{tt}"], writes=[f"cQ_a1{tt}"])
        S.add("vector", lambda E, tt=tt: E.tensor_tensor(
            qc(tt, 12), qc(tt, 6), qc(tt, 5), OP.subtract),
            reads=[f"cQ_m{tt}", f"cQ_min{tt}"], writes=[f"cQ_a2{tt}"])
        S.add("vector", lambda E, tt=tt: E.tensor_tensor(
            qc(tt, 11), qc(tt, 11), qc(tt, 12), OP.max),
            reads=[f"cQ_a1{tt}", f"cQ_a2{tt}"], writes=[f"cQ_a1{tt}"])
        S.add("vector", lambda E, tt=tt: E.scalar_tensor_tensor(
            qc(tt, 13), qc(tt, 11), QEPS, qc(tt, 10), OP.max, OP.mult),
            reads=[f"cQ_a1{tt}", f"cQ_rstd{tt}"], writes=[f"cQ_amx{tt}"])
        S.add("vector", lambda E, tt=tt: E.tensor_single_scalar(
            qc(tt, 13), qc(tt, 13), QEPS, OP.max),
            reads=[f"cQ_amx{tt}"], writes=[f"cQ_amx{tt}"])
        S.add("vector", lambda E, tt=tt: E.reciprocal(qc(tt, 14), qc(tt, 13)),
              reads=[f"cQ_amx{tt}"], writes=[f"cQ_ramx{tt}"])
        S.add("vector", lambda E, tt=tt: E.tensor_scalar(
            qc(tt, 15), qc(tt, 10), qc(tt, 14), 127.0, OP.mult, OP.mult),
            reads=[f"cQ_rstd{tt}", f"cQ_ramx{tt}"], writes=[f"qscl{tt}"])
        S.add("vector", lambda E, tt=tt: E.tensor_scalar(
            qc(tt, 7), qc(tt, 6), qc(tt, 15), -1.0, OP.mult, OP.mult),
            reads=[f"cQ_m{tt}", f"qscl{tt}"], writes=[f"qb2{tt}"])
        S.add("vector", lambda E, xt=xt, tt=tt: E.tensor_scalar(
            sqout, xt, qc(tt, 15), qc(tt, 7), OP.mult, OP.add),
            reads=[f"xseg{tt}", f"qscl{tt}", f"qb2{tt}"], writes=["sqout"])
        S.add("vector", lambda E: E.tensor_scalar(
            xn_tm[0], sqout, M64, M64, OP.add, OP.subtract),
            reads=["sqout"], writes=["xntm0"])
        S.add("vector", lambda E: E.tensor_scalar(
            sqout, sqout, M32, M32, OP.add, OP.subtract),
            reads=["sqout"], writes=["sqout"])
        S.add("vector", lambda E: E.scalar_tensor_tensor(
            xn_tm[1], sqout, dprime, xn_tm[0], OP.mult, OP.add),
            reads=["sqout", "dprime", "xntm0"], writes=["xntm1"])
        S.add("vector", lambda E, tt=tt: E.tensor_scalar(
            xn_tm[1], xn_tm[1], qc(tt, 13), 1.0 / 127.0, OP.mult, OP.mult),
            reads=["xntm1", f"cQ_amx{tt}"], writes=["xntm1"])
        for g in range(2):
            for i in range(4):
                kc = g * 4 + i
                S.add("tensor", lambda E, kc=kc, i=i: E.transpose(
                    psT[:, i * 256:i * 256 + 128],
                    xn_tm[1][:, kc * 128:(kc + 1) * 128], ident),
                    reads=["xntm1", "ident"], writes=["psT"])
            S.add("scalar", lambda E, s=s, tt=tt, g=g: E.copy(
                ap(xq_sb, g * 4 * T + s * 512 + tt * 128,
                   [[T, 4], [1, 128]]),
                ap(psT, 0, [[256, 4], [1, 128]])),
                reads=["psT"], writes=["xq"])
        if s + 1 < NSEG:
            x_load(s + 1, tt)
            ln1_reduce(s + 1, tt)

    for tt in range(4):
        x_load(0, tt)
    for tt in range(4):
        ln1_reduce(0, tt)

    for s in range(NSEG):
        ln1_cols_xn(s)
        qkv_proj(s)
        for tt in range(4):
            for bg in range(3):
                ncols = 396 if bg < 2 else 264
                S.add("scalar", lambda E, bg=bg, ncols=ncols: E.copy(
                    sbf3[bg][:, 0:ncols], psB[bg][:, 0:ncols]),
                    reads=[f"B{bg}"], writes=[f"sbf{bg}"])
            def stage1(P):
                b = PAIR_BANK[P]
                sl = PAIR_SLOT[P]
                yb = 3 if P % 2 == 0 else 4
                scb = 5 + (P % 2)
                qsl = phiQ[:, P * 512 + tt * 128:P * 512 + (tt + 1) * 128]
                ksl = phiK[:, P * 512 + tt * 128:P * 512 + (tt + 1) * 128]
                S.add("tensor", lambda E, qsl=qsl, b=b, sl=sl, yb=yb: E.matmul(
                    psB[yb][:, 0:132], qsl, sbf3[b][:, sl * 132:(sl + 1) * 132],
                    start=True, stop=False, skip_group_check=True),
                    reads=["phiQ", f"sbf{b}"], writes=[f"B{yb}"])
                S.add("tensor", lambda E, qsl=qsl, ksl=ksl, scb=scb: E.matmul(
                    psB[scb][:, 0:128], ksl[0:64, :], qsl[0:64, :],
                    start=True, stop=False, skip_group_check=True),
                    reads=["phiK", "phiQ"], writes=[f"B{scb}"])
                S.add("tensor", lambda E, qsl=qsl, ksl=ksl, scb=scb: E.matmul(
                    psB[scb][:, 128:256], ksl[64:128, :], qsl[64:128, :],
                    start=False, stop=True, skip_group_check=True),
                    reads=["phiK", "phiQ"], writes=[f"B{scb}"])
                S.add("tensor", lambda E, ksl=ksl, P=P: E.transpose(
                    psT[:, (P % 2) * 256:(P % 2) * 256 + 128], ksl, ident),
                    reads=["phiK", "ident"], writes=["psT"])
                S.add("scalar", lambda E, P=P: E.copy(
                    phiKT[P % 2], psT[:, (P % 2) * 256:(P % 2) * 256 + 128]),
                    reads=["psT"], writes=[f"phiKT{P % 2}"])

            def stage2(P):
                b = PAIR_BANK[P]
                sl = PAIR_SLOT[P]
                yb = 3 if P % 2 == 0 else 4
                scb = 5 + (P % 2)
                pp = P % 2
                S.add("vector", lambda E, pp=pp, scb=scb: E.tensor_tensor(
                    attnT[pp], psB[scb][:, 0:256], mask2, OP.mult),
                    reads=[f"B{scb}", "mask2"], writes=[f"attnT{pp}"])
                for h in range(2):
                    vs = vaug[:, tt * 1056 + P * 132 + h * 66:
                              tt * 1056 + P * 132 + h * 66 + 65]
                    S.add("tensor", lambda E, pp=pp, h=h, yb=yb, vs=vs: E.matmul(
                        psB[yb][:, h * 66:h * 66 + 65],
                        attnT[pp][:, h * 128:(h + 1) * 128], vs,
                        start=False, stop=(h == 1), skip_group_check=True),
                        reads=[f"attnT{pp}", "vaug"], writes=[f"B{yb}"])
                for h in range(2):
                    vs = vaug[:, tt * 1056 + P * 132 + h * 66:
                              tt * 1056 + P * 132 + h * 66 + 65]
                    stt = first_S_mm[b]
                    first_S_mm[b] = False
                    S.add("tensor", lambda E, pp=pp, h=h, b=b, sl=sl, vs=vs, stt=stt: E.matmul(
                        psB[b][h * 64:h * 64 + 64, sl * 132 + h * 66:sl * 132 + h * 66 + 65],
                        phiKT[pp][:, h * 64:(h + 1) * 64], vs,
                        start=stt, stop=False, skip_group_check=True,
                        tile_position=(0, h * 64)),
                        reads=[f"phiKT{pp}", "vaug"], writes=[f"B{b}"])
                S.add("vector", lambda E, pp=pp, yb=yb: E.tensor_single_scalar(
                    rdc[:, pp * 4:pp * 4 + 2],
                    ap(psB[yb], 64, [[66, 2]]), ZEPS, OP.max),
                    reads=[f"B{yb}"], writes=[f"rdc{pp}"])
                S.add("vector", lambda E, pp=pp: E.reciprocal(
                    rdc[:, pp * 4 + 2:pp * 4 + 4], rdc[:, pp * 4:pp * 4 + 2]),
                    reads=[f"rdc{pp}"], writes=[f"rdc{pp}"])
                for h in range(2):
                    S.add("vector", lambda E, pp=pp, yb=yb, h=h: E.tensor_scalar(
                        y_tm[pp][:, h * 64:(h + 1) * 64],
                        psB[yb][:, h * 66:h * 66 + 64],
                        rdc[:, pp * 4 + 2 + h:pp * 4 + 3 + h], None, OP.mult),
                        reads=[f"B{yb}", f"rdc{pp}"], writes=[f"ytm{pp}"])
                S.add("tensor", lambda E, pp=pp: E.transpose(
                    psT[:, 512 + pp * 256:512 + pp * 256 + 128], y_tm[pp], ident),
                    reads=[f"ytm{pp}", "ident"], writes=["psT"])
                S.add("scalar", lambda E, P=P, tt=tt, pp=pp: E.copy(
                    yT[:, P * 512 + tt * 128:P * 512 + (tt + 1) * 128],
                    psT[:, 512 + pp * 256:512 + pp * 256 + 128]),
                    reads=["psT"], writes=["xnT"])

            stage1(0)
            for P in range(1, 8):
                stage1(P)
                stage2(P - 1)
            stage2(7)
            emit_prepass(5)
            wo_and_tail(s, tt)
        emit_prepass(4)

    if t_xn is not None:
        S.add("sync", lambda E: E.dma_start(out=t_xn, in_=xnT),
              reads=["xnT"], dma=True)
    if t_phiq is not None:
        S.add("sync", lambda E: E.dma_start(out=t_phiq, in_=phiQ),
              reads=["phiQ"], dma=True)
    if t_phik is not None:
        S.add("sync", lambda E: E.dma_start(out=t_phik, in_=phiK),
              reads=["phiK"], dma=True)
    if t_vaug is not None:
        S.add("sync", lambda E: E.dma_start(out=bass.AP(
            t_vaug.tensor, 0, [[8 * 132, 128], [1, 4 * 8 * 132]]), in_=vaug),
            reads=["vaug"], dma=True)
    if t_y is not None:
        S.add("sync", lambda E: E.dma_start(out=t_y, in_=yT),
              reads=["xnT"], dma=True)
    if t_xq is not None:
        S.add("sync", lambda E: E.dma_start(out=t_xq, in_=xq_sb),
              reads=["xq"], dma=True)

    emit_prepass(192)
    emit_prepass_final()

    S.join()
    S.add("vector", lambda E: E.memset(maxacc, -3.0e38), writes=["maxacc"])
    S.add("vector", lambda E: E.memset(minacc, 3.0e38), writes=["minacc"])

    def wd_ternarize(jc):
        b = jc % 2
        S.add("gpsimd", lambda E, jc=jc, b=b: E.dma_start(
            out=wdbuf[b], in_=bass.AP(
                wd_d.tensor, jc * 128 * 1024, [[1024, 128], [1, 1024]])),
            writes=[f"wdbuf{b}"], dma=True)
        S.add("vector", lambda E, b=b: E.tensor_scalar(
            sqout, wdbuf[b], thr[:, 5:6], None, OP.is_le),
            reads=[f"wdbuf{b}", "thr"], writes=["sqout"])
        S.add("vector", lambda E, jc=jc, b=b: E.scalar_tensor_tensor(
            wdq_sb[:, jc * 1024:(jc + 1) * 1024], wdbuf[b], thr[:, 4:5],
            sqout, OP.is_ge, OP.subtract),
            reads=[f"wdbuf{b}", "thr", "sqout"], writes=["wdq"])

    for oc in range(32):
        wd_ternarize(oc)
        b = oc % 2
        S.add("sync", lambda E, oc=oc, b=b: E.dma_start(
            out=wstream[b][:, 0:1024], in_=bass.AP(
                wg_d.tensor, oc * 128 * 1024, [[1024, 128], [1, 1024]])),
            writes=[f"wstream{b}"], dma=True)
        S.add("sync", lambda E, oc=oc, b=b: E.dma_start(
            out=wstream[b][:, 1024:2048], in_=bass.AP(
                wu_d.tensor, oc * 128 * 1024, [[1024, 128], [1, 1024]])),
            writes=[f"wstream{b}"], dma=True)
        for (half, thri, wq2) in ((0, 0, wqg_sb), (1, 2, wqu_sb)):
            wsl = wstream[b][:, half * 1024:(half + 1) * 1024]
            S.add("vector", lambda E, wsl=wsl, thri=thri: E.tensor_scalar(
                sqout, wsl, thr[:, thri + 1:thri + 2], None, OP.is_le),
                reads=[f"wstream{b}", "thr"], writes=["sqout"])
            S.add("vector", lambda E, wsl=wsl, thri=thri, wq2=wq2, b=b: E.scalar_tensor_tensor(
                wq2[b], wsl, thr[:, thri:thri + 1], sqout, OP.is_ge, OP.subtract),
                reads=[f"wstream{b}", "thr", "sqout"], writes=[f"wq2_{id(wq2)}_{b}"])
        all_t8 = list(range(NSTRIP))
        t8_groups = [tuple(all_t8[i:i + 3]) for i in range(0, NSTRIP, 3)]
        for grp in t8_groups:
            ng = len(grp)
            gbanks = [0, 1, 2][:ng]
            ubanks = [3, 4, 5][:ng]
            for kc in range(8):
                for gi, t8 in enumerate(grp):
                    S.add("tensor", lambda E, kc=kc, b=b, gb=gbanks[gi], t8=t8: E.matmul(
                        psB[gb], wqg_sb[b][:, kc * 128:(kc + 1) * 128],
                        xq_sb[:, kc * T + t8 * 512:kc * T + (t8 + 1) * 512],
                        start=(kc == 0), stop=(kc == 7)),
                        reads=[f"wq2_{id(wqg_sb)}_{b}", "xq"], writes=[f"B{gbanks[gi]}"])
            for kc in range(8):
                for gi, t8 in enumerate(grp):
                    S.add("tensor", lambda E, kc=kc, b=b, ub=ubanks[gi], t8=t8: E.matmul(
                        psB[ub], wqu_sb[b][:, kc * 128:(kc + 1) * 128],
                        xq_sb[:, kc * T + t8 * 512:kc * T + (t8 + 1) * 512],
                        start=(kc == 0), stop=(kc == 7)),
                        reads=[f"wq2_{id(wqu_sb)}_{b}", "xq"], writes=[f"B{ubanks[gi]}"])
            for gi, t8 in enumerate(grp):
                gb, ub = gbanks[gi], ubanks[gi]
                pp = t8 % 2
                S.add("scalar", lambda E, gb=gb, pp=pp: E.activation(
                    g2_sb[pp], psB[gb], AF.Silu, scale=mw[:, 0:1]),
                    reads=[f"B{gb}", "mw"], writes=[f"g2_{pp}"])
                S.add("scalar", lambda E, ub=ub, pp=pp: E.copy(ud_sb[pp], psB[ub]),
                      reads=[f"B{ub}"], writes=[f"ud_{pp}"])
                S.add("vector", lambda E, pp=pp: E.tensor_tensor(
                    hbf_sb[pp], g2_sb[pp], ud_sb[pp], OP.mult),
                    reads=[f"g2_{pp}", f"ud_{pp}"], writes=[f"hbf{pp}"])
                S.add("vector", lambda E, pp=pp, t8=t8: E.tensor_tensor(
                    maxacc[:, t8 * 512:(t8 + 1) * 512],
                    maxacc[:, t8 * 512:(t8 + 1) * 512], hbf_sb[pp], OP.max),
                    reads=[f"hbf{pp}", "maxacc"], writes=["maxacc"])
                S.add("vector", lambda E, pp=pp, t8=t8: E.tensor_tensor(
                    minacc[:, t8 * 512:(t8 + 1) * 512],
                    minacc[:, t8 * 512:(t8 + 1) * 512], hbf_sb[pp], OP.min),
                    reads=[f"hbf{pp}", "minacc"], writes=["minacc"])
                for hh in range(2):
                    S.add("sync", lambda E, oc=oc, t8=t8, pp=pp, hh=hh: E.dma_start(
                        out=bass.AP(h_d.tensor,
                                    ((2 * t8 + hh) * INNER + oc * 128) * 256,
                                    [[256, 128], [1, 256]]),
                        in_=hbf_sb[pp][:, hh * 256:(hh + 1) * 256]),
                        reads=[f"hbf{pp}"], dma=True)
    S.add("vector", lambda E: E.tensor_single_scalar(minacc, minacc, -1.0, OP.mult),
          reads=["minacc"], writes=["minacc"])
    S.add("vector", lambda E: E.tensor_tensor(mxabs, maxacc, minacc, OP.max),
          reads=["maxacc", "minacc"], writes=["mxabs"])
    for c in range(NT128):
        sl = c % 4
        S.add("tensor", lambda E, c=c, sl=sl: E.transpose(
            psT[:, sl * 256:sl * 256 + 128], mxabs[:, c * 128:(c + 1) * 128], ident),
            reads=["mxabs", "ident"], writes=["psT"])
        S.add("vector", lambda E, c=c, sl=sl: E.tensor_reduce(
            mxw[:, c:c + 1], psT[:, sl * 256:sl * 256 + 128], AX.X, OP.max),
            reads=["psT"], writes=["mxw"])
    S.add("vector", lambda E: E.tensor_scalar(
        betac, mxw, mw[:, 1:2], QEPS, OP.mult, OP.max),
        reads=["mxw", "mw"], writes=["acl"])
    S.add("vector", lambda E: E.reciprocal(mxw, betac),
          reads=["acl"], writes=["rac"])
    S.add("vector", lambda E: E.tensor_scalar(
        mxw, mxw, mw[:, 1:2], 127.0, OP.mult, OP.mult),
        reads=["rac", "mw"], writes=["qscols"])
    S.add("vector", lambda E: E.tensor_scalar(
        betac, betac, mw[:, 2:3], 1.0 / 127.0, OP.mult, OP.mult),
        reads=["acl", "mw"], writes=["betac"])

    if t_wdq is not None:
        S.add("sync", lambda E: E.dma_start(out=t_wdq, in_=wdq_sb),
              reads=["wdq"], dma=True)
    if t_mx is not None:
        S.add("sync", lambda E: E.dma_start(out=t_mx, in_=mxw),
              reads=["qscols"], dma=True)

    S.join()

    def b2_strip_load(hs):
        buf = hs % 2
        for g in range(2):
            S.add("sync", lambda E, hs=hs, g=g, buf=buf: E.dma_start(
                out=ap(hstrip[buf], g * 16 * 256, [[256, 16], [1, 256]]),
                in_=bass.AP(h_d.tensor, (hs * INNER + g * 16 * 128) * 256,
                            [[256, 128], [128 * 256, 16], [1, 256]])),
                writes=[f"hstrip{buf}"], dma=True)

    def b2_quant(tc):
        c = tc % 2
        hs = tc // 2
        buf = hs % 2
        hq = hq_sb[tc % 2]
        S.add("tensor", lambda E, tc=tc: E.transpose(
            bass.AP(psB[6].tensor, psB[6].offset, [[512, 1], [1, 128]]),
            mxw[:, tc:tc + 1], identf),
            reads=["qscols", "identf"], writes=["B6"])
        S.add("scalar", lambda E, tc=tc: E.copy(
            qrow1[tc % 2], bass.AP(psB[6].tensor, psB[6].offset, [[512, 1], [1, 128]])),
            reads=["B6"], writes=[f"qrow1{tc % 2}"])
        S.add("tensor", lambda E, tc=tc: E.matmul(
            psB[6][:, 0:128], onerow_f, qrow1[tc % 2], start=True, stop=True),
            reads=["onerow", f"qrow1{tc % 2}"], writes=["B6"])
        S.add("scalar", lambda E, tc=tc: E.copy(qsb[tc % 2], psB[6][:, 0:128]),
              reads=["B6"], writes=[f"qsb{tc % 2}"])
        for qq in range(4):
            hsl = ap(hstrip[buf], qq * 8 * 256 + c * 128, [[256, 8], [1, 128]])
            S.add("gpsimd", lambda E, hsl=hsl, tc=tc: E.tensor_tensor(
                ap(qy, 0, [[128, 8], [1, 128]]), hsl,
                ap(qsb[tc % 2], 0, [[0, 8], [1, 128]]), OP.mult),
                reads=[f"hstrip{buf}", f"qsb{tc % 2}"], writes=["qy"])
            S.add("vector", lambda E: E.tensor_scalar(
                qC, qy, M64, M64, OP.add, OP.subtract),
                reads=["qy"], writes=["qC"])
            S.add("vector", lambda E: E.tensor_scalar(
                qy, qy, M32, M32, OP.add, OP.subtract),
                reads=["qy"], writes=["qy"])
            S.add("vector", lambda E, hq=hq, qq=qq, tc=tc: E.scalar_tensor_tensor(
                hq[:, qq * 1024:(qq + 1) * 1024], qy, dprime, qC,
                OP.mult, OP.add),
                reads=["qy", "dprime", "qC"], writes=[f"hq{tc % 2}"])
        S.add("sync", lambda E, tc=tc: E.dma_start(
            out=x1ld[tc % 2], in_=bass.AP(x1_d.tensor, tc * 128 * D,
                                          [[D, 128], [1, D]])),
            writes=[f"x1ld{tc % 2}"], dma=True)

    def b2_mm(tc):
        hq = hq_sb[tc % 2]
        bA, bB = (3, 4) if tc % 2 == 0 else (5, 2)
        for jo in range(32):
            for oh, bank in ((0, bA), (1, bB)):
                S.add("tensor", lambda E, jo=jo, oh=oh, bank=bank, hq=hq, tc=tc: E.matmul(
                    psB[bank], hq[:, jo * 128:(jo + 1) * 128],
                    wdq_sb[:, jo * D + oh * 512:jo * D + (oh + 1) * 512],
                    start=(jo == 0), stop=(jo == 31)),
                    reads=[f"hq{tc % 2}", "wdq"], writes=[f"B{bank}"])
        for oh, bank in ((0, bA), (1, bB)):
            S.add("vector", lambda E, oh=oh, bank=bank, tc=tc: E.scalar_tensor_tensor(
                outt[tc % 2][:, oh * 512:(oh + 1) * 512], psB[bank],
                betac[:, tc:tc + 1], x1ld[tc % 2][:, oh * 512:(oh + 1) * 512],
                OP.mult, OP.add),
                reads=[f"B{bank}", "betac", f"x1ld{tc % 2}"],
                writes=[f"outt{tc % 2}"])
        S.add("sync", lambda E, tc=tc: E.dma_start(
            out=bass.AP(out_d.tensor, tc * 128 * D, [[D, 128], [1, D]]),
            in_=outt[tc % 2]),
            reads=[f"outt{tc % 2}"], dma=True)

    b2_strip_load(0)
    b2_strip_load(1)
    for tc in range(NT128):
        b2_quant(tc)
        if tc % 2 == 1 and tc // 2 + 2 < NHSTRIP:
            b2_strip_load(tc // 2 + 2)
        if tc >= 1:
            b2_mm(tc - 1)
    b2_mm(NT128 - 1)

    fin_waits = [(c, S.ctr[c]) for c in DMA_LANES if S.ctr[c] > 0]
    S.add("vector", lambda E: E.memset(redcol, 0.0),
          waits=fin_waits, writes=["redcol"])

    S.emit(nc, sems)
    ctx.close()
    return nc, taps


def _host_layouts(W_q, W_k, W_v, W_o, w_gate, w_up, w_down):
    def attw(W):
        return np.ascontiguousarray(
            W.T.reshape(8, 128, 1024).transpose(1, 0, 2).reshape(128, 8192))

    def ffnw(W):
        g = W.T.reshape(8, 128, 32, 128)
        return np.ascontiguousarray(g.transpose(2, 1, 0, 3).reshape(32, 128, 1024))

    def dw(W):
        return np.ascontiguousarray(W.T.reshape(32, 128, 1024))

    return (attw(W_q), attw(W_k), attw(W_v), attw(W_o),
            ffnw(w_gate), ffnw(w_up), dw(w_down))


_NC_CACHE = {}


def kernel(L0_attn_x, L0_attn_W_q, L0_attn_W_k, L0_attn_W_v, L0_attn_W_o,
           L0_ffn_w_gate, L0_ffn_w_up, L0_ffn_w_down):
    x = np.asarray(L0_attn_x, dtype=np.float32)
    B, T, _ = x.shape
    wq, wk, wv, wo, wg, wu, wd = _host_layouts(
        np.asarray(L0_attn_W_q, np.float32), np.asarray(L0_attn_W_k, np.float32),
        np.asarray(L0_attn_W_v, np.float32), np.asarray(L0_attn_W_o, np.float32),
        np.asarray(L0_ffn_w_gate, np.float32), np.asarray(L0_ffn_w_up, np.float32),
        np.asarray(L0_ffn_w_down, np.float32))
    key = T
    if key not in _NC_CACHE:
        _NC_CACHE[key] = build_program(T=T)[0]
    nc = _NC_CACHE[key]
    in_maps = []
    for c in range(B):
        in_maps.append({
            "x": np.ascontiguousarray(x[c]),
            "wqT": wq, "wkT": wk, "wvT": wv, "woT": wo,
            "wgT": wg, "wuT": wu, "wdT": wd,
        })
    res = run_bass_kernel_spmd(nc, in_maps, core_ids=list(range(B)))
    return np.stack([res.results[c]["out"] for c in range(B)], axis=0)


# version 17
# speedup vs baseline: 1.2755x; 1.0193x over previous
import numpy as np
import ml_dtypes
import concourse.bass as bass
import concourse.mybir as mybir
import bass_rust as _br
from concourse.bass_utils import run_bass_kernel_spmd
from contextlib import ExitStack

F32 = mybir.dt.float32
BF16 = mybir.dt.bfloat16
OP = mybir.AluOpType
AF = mybir.ActivationFunctionType
AX = mybir.AxisListType

M32 = 12582912.0
M64 = 6755399441055744.0

D = 1024
INNER = 4096
SEG = 512
LNEPS = 1e-5
ZEPS = 1e-6
QEPS = 1e-5

ENGINES = ("sync", "tensor", "vector", "scalar", "gpsimd")
ENG_CTR = {"sync": "sS", "tensor": "tS", "vector": "vS", "scalar": "cS",
           "gpsimd": "gS"}
DMA_CTR = {"sync": "qsx", "gpsimd": "qgp", "scalar": "qsc"}
NLANES = 6
DMA_LANES = [f"{q}{l}" for q in ("qsx", "qgp", "qsc") for l in range(NLANES)]
ALL_CTRS = tuple(DMA_LANES) + ("sS", "tS", "vS", "cS", "gS")


class Sched:
    def __init__(self):
        self.ops = {e: [] for e in ENGINES}
        self.ctr = {c: 0 for c in ALL_CTRS}
        self.dma_i = {"qsx": 0, "qgp": 0, "qsc": 0}
        self.res = {}
        self.mark = {e: {} for e in ENGINES}
        self.pending_join = {e: None for e in ENGINES}

    def _r(self, name):
        if name not in self.res:
            self.res[name] = {"w": None, "r": []}
        return self.res[name]

    def add(self, eng, fn, reads=(), writes=(), waits=(), dma=False):
        if dma:
            q = DMA_CTR[eng]
            i = self.dma_i[q]
            self.dma_i[q] = i + 1
            cname = f"{q}{i % NLANES}"
            step = 16
            lane_reuse = (cname, 16 * (i // NLANES)) if i >= NLANES else None
        else:
            cname = ENG_CTR[eng]
            step = 1
            lane_reuse = None
        wl = {}

        def need(dep):
            if dep is None:
                return
            c, v = dep
            if v <= 0 or self.mark[eng].get(c, 0) >= v:
                return
            wl[c] = max(wl.get(c, 0), v)

        if lane_reuse is not None:
            need(lane_reuse)
        jw = self.pending_join[eng]
        if jw:
            for c, v in jw.items():
                need((c, v))
            self.pending_join[eng] = None
        for c, v in waits:
            need((c, v))
        for rn in reads:
            need(self._r(rn)["w"])
        for wn in writes:
            r = self._r(wn)
            need(r["w"])
            for dep in r["r"]:
                need(dep)
        self.ctr[cname] += step
        me = (cname, self.ctr[cname])
        for rn in reads:
            self._r(rn)["r"].append(me)
        for wn in writes:
            r = self._r(wn)
            r["w"] = me
            r["r"] = []
        for c, v in wl.items():
            assert v <= self.ctr[c], f"future wait {c}={v} now={self.ctr[c]}"
            self.mark[eng][c] = max(self.mark[eng].get(c, 0), v)
        self.ops[eng].append((tuple(sorted(wl.items())), fn, cname, step))
        return me

    def join(self):
        snap = dict(self.ctr)
        for e in ENGINES:
            self.pending_join[e] = dict(snap)
            self.mark[e] = {}

    def emit(self, nc, sems):
        with nc.Block() as blk:
            for eng in ENGINES:
                ops = self.ops[eng]

                def body(E, ops=ops):
                    for waits, fn, cname, step in ops:
                        for (c, v) in waits:
                            E.wait_ge(sems[c], v)
                        inst = fn(E)
                        inst.then_inc(sems[cname], step)
                getattr(blk, eng)(body)


def ap(t, off, dims):
    p = [list(d) for d in t.ap][0]
    return bass.AP(t.tensor, t.offset + off, [p] + [list(d) for d in dims])


def build_program(T=4096, debug_taps=()):
    NSEG = T // SEG
    NT128 = T // 128
    NSTRIP = T // 512
    NHSTRIP = T // 256

    nc = bass.Bass("TRN2", target_bir_lowering=False, debug=False)

    x_d = nc.dram_tensor("x", [T, D], F32, kind="ExternalInput").ap()
    wq_d = nc.dram_tensor("wqT", [128, 8 * D], F32, kind="ExternalInput").ap()
    wk_d = nc.dram_tensor("wkT", [128, 8 * D], F32, kind="ExternalInput").ap()
    wv_d = nc.dram_tensor("wvT", [128, 8 * D], F32, kind="ExternalInput").ap()
    wo_d = nc.dram_tensor("woT", [128, 8 * D], F32, kind="ExternalInput").ap()
    wg_d = nc.dram_tensor("wgT", [32, 128, 8 * 128], F32, kind="ExternalInput").ap()
    wu_d = nc.dram_tensor("wuT", [32, 128, 8 * 128], F32, kind="ExternalInput").ap()
    wd_d = nc.dram_tensor("wdT", [32, 128, D], F32, kind="ExternalInput").ap()
    out_d = nc.dram_tensor("out", [T, D], F32, kind="ExternalOutput").ap()
    x1_d = nc.dram_tensor("x1_scr", [T, D], F32, kind="Internal").ap()
    h_d = nc.dram_tensor("h_scr", [NHSTRIP, INNER, 256], BF16, kind="Internal").ap()

    taps = {}

    def tap(name, shape, dtype=F32):
        if name in debug_taps:
            taps[name] = nc.dram_tensor("tap_" + name, shape, dtype,
                                        kind="ExternalOutput").ap()
        return taps.get(name)

    t_xn = tap("xn", [128, 8 * SEG], BF16)
    t_phiq = tap("phiq", [128, 8 * SEG], BF16)
    t_phik = tap("phik", [128, 8 * SEG], BF16)
    t_vaug = tap("vaug", [4, 128, 8 * 132], BF16)
    t_y = tap("y", [128, 8 * SEG], BF16)
    t_xq = tap("xq", [128, 8 * T], BF16)
    t_wdq = tap("wdq", [128, 32 * D], BF16)
    t_mx = tap("mx", [128, NT128])

    ctx = ExitStack()
    sb = lambda name, shape, dt=F32: ctx.enter_context(
        nc.sbuf_tensor(name, shape, dt)).ap()
    psb = lambda name, shape, dt=F32: ctx.enter_context(
        nc.psum_tensor(name, shape, dt)).ap()

    iot = sb("iot", [128, 128])
    ident = sb("ident", [128, 128], BF16)
    identf = sb("identf", [128, 128])
    mask2 = sb("mask2", [128, 256], BF16)
    onecol = sb("onecol", [128, 1])
    epscol = sb("epscol", [128, 1])
    nhalfcol = sb("nhalfcol", [128, 1])
    onerow_f = sb("onerow_f", [1, 128])
    onesmat = sb("onesmat", [128, 128])
    disc = sb("disc", [128, 1])
    dprime = sb("dprime", [128, 1])
    mw = sb("mw", [128, 4])
    thr = sb("thr", [128, 6])
    wsum = sb("wsum", [128, 4])
    redcol = sb("redcol", [128, 1])
    bigacc = sb("bigacc", [128, 384])
    colsA = sb("colsA", [128, 32])
    colsQ = sb("colsQ", [128, 64])
    rdc = sb("rdc", [128, 8])
    mxw = sb("mxw", [128, NT128])
    qrow1 = [sb(f"qrow1_{i}", [1, 128]) for i in range(2)]
    betac = sb("betac", [128, NT128])

    A1 = sb("arena1", [128, 4 * 8 * D], BF16)
    A2 = sb("arena2", [128, 22528], BF16)
    A3 = sb("arena3", [128, 7424])
    XQ = sb("xq_arena", [128, max(8 * T, 24576)], BF16)

    wq_sb = A1[:, 0:8192]
    wk_sb = A1[:, 8192:16384]
    wv_sb = A1[:, 16384:24576]
    wo_sb = A1[:, 24576:32768]
    wdq_sb = A1[:, 0:32768]

    xnT = A2[:, 0:4096]
    phiQ = A2[:, 4096:8192]
    phiK = A2[:, 8192:12288]
    yT = A2[:, 0:4096]
    vaug = A2[:, 12288:16512]
    xn_tm = [A2[:, 16512 + i * 1024:16512 + (i + 1) * 1024] for i in range(2)]
    attnT = [A2[:, 18560 + i * 256:18560 + (i + 1) * 256] for i in range(2)]
    phiKT = [A2[:, 19072 + i * 128:19072 + (i + 1) * 128] for i in range(2)]
    sbf3 = [A2[:, 19328 + i * 396:19328 + (i + 1) * 396] for i in range(3)]
    y_tm = [A2[:, 20516 + i * 128:20516 + (i + 1) * 128] for i in range(2)]
    maxacc = A2[:, 0:T]
    minacc = A2[:, T:2 * T]
    wqg_sb = [A2[:, 8192 + i * 1024:8192 + (i + 1) * 1024] for i in range(2)]
    wqu_sb = [A2[:, 10240 + i * 1024:10240 + (i + 1) * 1024] for i in range(2)]
    g2_sb = [A2[:, 12288 + i * 512:12288 + (i + 1) * 512] for i in range(2)]
    ud_sb = [A2[:, 13312 + i * 512:13312 + (i + 1) * 512] for i in range(2)]
    hbf_sb = [A2[:, 14336 + i * 512:14336 + (i + 1) * 512] for i in range(2)]
    mxabs = A2[:, 18432:18432 + T]

    xseg = A3[:, 0:4096]
    ptmp = [A3[:, 4096 + i * 512:4096 + (i + 1) * 512] for i in range(2)]
    meanbuf = [A3[:, 5376 + i * 256:5376 + (i + 1) * 256] for i in range(4)]
    sqout = A3[:, 6400:7424]
    wstream = [A3[:, i * 2048:(i + 1) * 2048] for i in range(2)]
    wdbuf = [A3[:, 4096 + i * 1024:4096 + (i + 1) * 1024] for i in range(2)]
    meanbig = [A3[:, 5376 + i * 512:5376 + (i + 1) * 512] for i in range(2)]
    qy = A3[:, 0:1024]
    qC = A3[:, 1024:2048]
    x1ld = [A3[:, 2048 + i * 1024:2048 + (i + 1) * 1024] for i in range(2)]
    outt = [A3[:, 4096 + i * 1024:4096 + (i + 1) * 1024] for i in range(2)]
    qsb = [A3[:, 6144 + i * 128:6144 + (i + 1) * 128] for i in range(2)]

    xq_sb = XQ[:, 0:8 * T]
    hstrip = [XQ[:, i * 8192:(i + 1) * 8192] for i in range(2)]
    hq_sb = [XQ[:, 16384 + i * 4096:16384 + (i + 1) * 4096] for i in range(2)]

    psB = [psb(f"psB{i}", [128, 512]) for i in range(7)]
    psT = psb("psT", [128, 1024], BF16)

    sems = {}
    for n in ALL_CTRS:
        sems[n] = ctx.enter_context(nc.semaphore("sem_" + n))

    S = Sched()

    S.add("gpsimd", lambda E: E.iota(iot, [[1, 128]], channel_multiplier=-1,
                                     allow_small_or_imprecise_dtypes=True),
          writes=["iot"])
    S.add("vector", lambda E: E.tensor_single_scalar(ident, iot, 0.0, OP.is_equal),
          reads=["iot"], writes=["ident"])
    S.add("vector", lambda E: E.tensor_single_scalar(identf, iot, 0.0, OP.is_equal),
          reads=["iot"], writes=["identf"])
    S.add("vector", lambda E: E.tensor_single_scalar(
        ap(mask2, 0, [[128, 2], [1, 128]]),
        ap(iot, 0, [[0, 2], [1, 128]]), 0.0, OP.is_ge),
        reads=["iot"], writes=["mask2"])
    S.add("vector", lambda E: E.memset(onecol, 1.0), writes=["onecol"])
    S.add("vector", lambda E: E.memset(epscol, LNEPS), writes=["epscol"])
    S.add("vector", lambda E: E.memset(nhalfcol, -0.5), writes=["nhalfcol"])
    S.add("vector", lambda E: E.memset(onerow_f, 1.0), writes=["onerow"])
    S.add("vector", lambda E: E.memset(onesmat, 1.0), writes=["onesmat"])
    S.add("vector", lambda E: E.tensor_scalar(disc, onecol, M64, M64,
                                              OP.add, OP.subtract),
          reads=["onecol"], writes=["disc"])
    S.add("vector", lambda E: E.tensor_scalar(dprime, disc, -1.0, 1.0,
                                              OP.mult, OP.add),
          reads=["disc"], writes=["dprime"])
    for b in range(3):
        S.add("vector", lambda E, b=b: E.memset(psB[b], 0.0), writes=[f"B{b}"])
    S.add("vector", lambda E: E.memset(vaug, 0.0), writes=["vaug"])
    S.add("vector", lambda E: E.memset(
        ap(vaug, 64, [[1056, 4], [132, 8], [66, 2]]), 1.0),
        writes=["vaug"])

    for (dst, src, nm) in ((wq_sb, wq_d, "wq"), (wk_sb, wk_d, "wk"),
                           (wv_sb, wv_d, "wv"), (wo_sb, wo_d, "wo")):
        S.add("gpsimd", lambda E, d=dst, s=src: E.dma_start(out=d, in_=s),
              writes=[nm], dma=True)

    S.add("vector", lambda E: E.memset(bigacc, 0.0), writes=["bigacc"])
    mean_srcs = []
    for oc in range(32):
        mean_srcs.append((wg_d, oc * 128 * 1024))
    for oc in range(32):
        mean_srcs.append((wu_d, oc * 128 * 1024))
    for jc in range(32):
        mean_srcs.append((wd_d, jc * 128 * 1024))
    mean_srcs = mean_srcs[0:32][::1] + mean_srcs[32:64] + mean_srcs[64:96]
    _prep = {"i": 0}

    def emit_prepass(n):
        while n > 0 and _prep["i"] < 192:
            i = _prep["i"]
            _prep["i"] += 1
            n -= 1
            srct, off = mean_srcs[i // 2]
            half = i % 2
            b = i % 2
            sub = bass.AP(srct.tensor, off + half * 512, [[1024, 128], [1, 512]])
            S.add("sync", lambda E, b=b, s=sub: E.dma_start(out=meanbig[b], in_=s),
                  writes=[f"meanbig{b}"], dma=True)
            S.add("scalar", lambda E, b=b, i=i: E.activation(
                meanbig[b], meanbig[b], AF.Abs, accum_out=bigacc[:, i:i + 1]),
                reads=[f"meanbig{b}", "bigacc"], writes=[f"meanbig{b}", f"bigacc{i}"])

    def emit_prepass_final():
        for m in range(3):
            S.add("tensor", lambda E, m=m: E.matmul(
                psB[6][:, 0:64], onesmat, bigacc[:, m * 64:(m + 1) * 64],
                start=True, stop=True),
                reads=["onesmat"] + [f"bigacc{i}" for i in range(m * 64, (m + 1) * 64)],
                writes=["B6"])
            S.add("vector", lambda E, m=m: E.tensor_reduce(
                mw[:, m:m + 1], psB[6][:, 0:64], AX.X, OP.add),
                reads=["B6"], writes=["mwraw"])
        S.add("vector", lambda E: E.tensor_scalar(mw[:, 0:3], mw[:, 0:3],
                                                  1.0 / (INNER * D), QEPS,
                                                  OP.mult, OP.max),
              reads=["mwraw"], writes=["mw"])
        S.add("vector", lambda E: E.tensor_scalar(
            ap(thr, 0, [[2, 3], [1, 1]]), mw[:, 0:3], 0.5, None, OP.mult),
            reads=["mw"], writes=["thr"])
        S.add("vector", lambda E: E.tensor_scalar(
            ap(thr, 1, [[2, 3], [1, 1]]), mw[:, 0:3], -0.5, None, OP.mult),
            reads=["mw"], writes=["thr"])

    PAIR_BANK = {0: 0, 1: 0, 2: 0, 3: 1, 4: 1, 5: 1, 6: 2, 7: 2}
    PAIR_SLOT = {0: 0, 1: 1, 2: 2, 3: 0, 4: 1, 5: 2, 6: 0, 7: 1}
    first_S_mm = [True, True, True]

    def qc(tt, k):
        return colsQ[:, tt * 16 + k:tt * 16 + k + 1]

    def x_load(s, tt):
        S.add("sync", lambda E, s=s, tt=tt: E.dma_start(
            out=xseg[:, tt * D:(tt + 1) * D],
            in_=bass.AP(x_d.tensor, (s * SEG + tt * 128) * D,
                        [[D, 128], [1, D]])),
            writes=[f"xseg{tt}"], dma=True)

    def ln1_reduce(s, tt):
        xt = xseg[:, tt * D:(tt + 1) * D]
        S.add("scalar", lambda E, xt=xt, tt=tt: E.activation(
            sqout, xt, AF.Square, accum_out=colsA[:, 4 + tt:5 + tt]),
            reads=[f"xseg{tt}"], writes=["sqout", "cA_sq"])
        S.add("vector", lambda E, xt=xt, tt=tt: E.tensor_reduce(
            colsA[:, tt:tt + 1], xt, AX.X, OP.add),
            reads=[f"xseg{tt}"], writes=["cA_sum"])

    def ln1_cols_xn(s):
        S.add("vector", lambda E: E.tensor_scalar(
            colsA[:, 8:12], colsA[:, 0:4], 1.0 / D, None, OP.mult),
            reads=["cA_sum"], writes=["cA_m"])
        S.add("vector", lambda E: E.tensor_tensor(
            colsA[:, 12:16], colsA[:, 8:12], colsA[:, 8:12], OP.mult),
            reads=["cA_m"], writes=["cA_m2"])
        S.add("vector", lambda E: E.scalar_tensor_tensor(
            colsA[:, 16:20], colsA[:, 4:8], 1.0 / D, colsA[:, 12:16],
            OP.mult, OP.subtract),
            reads=["cA_sq", "cA_m2"], writes=["cA_var"])
        S.add("scalar", lambda E: E.activation(
            colsA[:, 20:24], colsA[:, 16:20], AF.Ln, bias=epscol),
            reads=["cA_var", "epscol"], writes=["cA_ln"])
        S.add("scalar", lambda E: E.activation(
            colsA[:, 24:28], colsA[:, 20:24], AF.Exp, scale=nhalfcol),
            reads=["cA_ln", "nhalfcol"], writes=["cA_rstd"])
        S.add("vector", lambda E: E.scalar_tensor_tensor(
            colsA[:, 28:32], colsA[:, 8:12], -1.0, colsA[:, 24:28],
            OP.mult, OP.mult),
            reads=["cA_m", "cA_rstd"], writes=["cA_mb"])
        for tt in range(4):
            xt = xseg[:, tt * D:(tt + 1) * D]
            S.add("vector", lambda E, xt=xt, tt=tt: E.tensor_scalar(
                xn_tm[tt % 2], xt, colsA[:, 24 + tt:25 + tt],
                colsA[:, 28 + tt:29 + tt], OP.mult, OP.add),
                reads=[f"xseg{tt}", "cA_rstd", "cA_mb"], writes=[f"xntm{tt % 2}"])
            for g in range(2):
                for i in range(4):
                    ic = g * 4 + i
                    S.add("tensor", lambda E, tt=tt, ic=ic, i=i: E.transpose(
                        psT[:, i * 256:i * 256 + 128],
                        xn_tm[tt % 2][:, ic * 128:(ic + 1) * 128], ident),
                        reads=[f"xntm{tt % 2}", "ident"], writes=["psT"])
                S.add("scalar", lambda E, tt=tt, g=g: E.copy(
                    ap(xnT, g * 4 * 512 + tt * 128, [[512, 4], [1, 128]]),
                    ap(psT, 0, [[256, 4], [1, 128]])),
                    reads=["psT"], writes=["xnT"])

    def qkv_proj(s):
        for (which, wsb, wres, phidst, phin) in (
                ("q", wq_sb, "wq", phiQ, "phiQ"), ("k", wk_sb, "wk", phiK, "phiK")):
            for og in range(4):
                b0 = 3 + (og % 2) * 2
                for kc in range(8):
                    for ob in range(2):
                        oc = og * 2 + ob
                        bank = b0 + ob
                        S.add("tensor", lambda E, oc=oc, kc=kc, bank=bank, wsb=wsb: E.matmul(
                            psB[bank], wsb[:, kc * D + oc * 128:kc * D + oc * 128 + 128],
                            xnT[:, kc * 512:(kc + 1) * 512],
                            start=(kc == 0), stop=(kc == 7)),
                            reads=[wres, "xnT"], writes=[f"B{bank}"])
                for ob in range(2):
                    oc = og * 2 + ob
                    bank = b0 + ob
                    S.add("vector", lambda E, oc=oc, bank=bank: E.tensor_single_scalar(
                        ptmp[oc % 2], psB[bank], 0.0, OP.min),
                        reads=[f"B{bank}"], writes=[f"ptmp{oc % 2}"])
                    S.add("scalar", lambda E, oc=oc: E.activation(
                        ptmp[oc % 2], ptmp[oc % 2], AF.Exp),
                        reads=[f"ptmp{oc % 2}"], writes=[f"ptmp{oc % 2}"])
                    S.add("vector", lambda E, oc=oc, bank=bank, phidst=phidst, phin=phin: E.scalar_tensor_tensor(
                        phidst[:, oc * 512:(oc + 1) * 512], psB[bank], 0.0,
                        ptmp[oc % 2], OP.max, OP.add),
                        reads=[f"B{bank}", f"ptmp{oc % 2}"], writes=[phin])
        for tt in range(4):
            b0 = 3 + (tt % 2) * 2
            for ic in range(8):
                for oh in range(2):
                    bank = b0 + oh
                    S.add("tensor", lambda E, tt=tt, oh=oh, ic=ic, bank=bank: E.matmul(
                        psB[bank], xnT[:, ic * 512 + tt * 128:ic * 512 + (tt + 1) * 128],
                        wv_sb[:, ic * D + oh * 512:ic * D + (oh + 1) * 512],
                        start=(ic == 0), stop=(ic == 7)),
                        reads=["xnT", "wv"], writes=[f"B{bank}"])
            for oh in range(2):
                bank = b0 + oh
                S.add("scalar", lambda E, tt=tt, oh=oh, bank=bank: E.copy(
                    ap(vaug, tt * 1056 + oh * 4 * 132,
                       [[132, 4], [1, 64]]),
                    ap(psB[bank], 0, [[128, 4], [1, 64]])),
                    reads=[f"B{bank}"], writes=["vaug"])
                S.add("scalar", lambda E, tt=tt, oh=oh, bank=bank: E.copy(
                    ap(vaug, tt * 1056 + oh * 4 * 132 + 66,
                       [[132, 4], [1, 64]]),
                    ap(psB[bank], 64, [[128, 4], [1, 64]])),
                    reads=[f"B{bank}"], writes=["vaug"])

    def wo_and_tail(s, tt):
        b0w = 3 + (tt % 2) * 2
        for dc in range(8):
            for oh in range(2):
                bank = b0w + oh
                S.add("tensor", lambda E, tt=tt, oh=oh, dc=dc, bank=bank: E.matmul(
                    psB[bank], yT[:, dc * 512 + tt * 128:dc * 512 + (tt + 1) * 128],
                    wo_sb[:, dc * D + oh * 512:dc * D + (oh + 1) * 512],
                    start=(dc == 0), stop=(dc == 7)),
                    reads=["xnT", "wo"], writes=[f"B{bank}"])
        for oh in range(2):
            bank = b0w + oh
            S.add("vector", lambda E, tt=tt, oh=oh, bank=bank: E.scalar_tensor_tensor(
                xseg[:, tt * D + oh * 512:tt * D + (oh + 1) * 512],
                psB[bank], 1.0,
                xseg[:, tt * D + oh * 512:tt * D + (oh + 1) * 512],
                OP.mult, OP.add, accum_out=qc(tt, oh)),
                reads=[f"B{bank}", f"xseg{tt}"],
                writes=[f"xseg{tt}", f"cQ_s{tt}"])
        S.add("sync", lambda E, s=s, tt=tt: E.dma_start(
            out=bass.AP(x1_d.tensor, (s * SEG + tt * 128) * D,
                        [[D, 128], [1, D]]),
            in_=xseg[:, tt * D:(tt + 1) * D]),
            reads=[f"xseg{tt}"], dma=True)
        xt = xseg[:, tt * D:(tt + 1) * D]
        S.add("vector", lambda E, tt=tt: E.tensor_tensor(
            qc(tt, 2), qc(tt, 0), qc(tt, 1), OP.add),
            reads=[f"cQ_s{tt}"], writes=[f"cQ_sum{tt}"])
        S.add("scalar", lambda E, xt=xt, tt=tt: E.activation(
            sqout, xt, AF.Square, accum_out=qc(tt, 3)),
            reads=[f"xseg{tt}"], writes=["sqout", f"cQ_sq{tt}"])
        S.add("vector", lambda E, xt=xt, tt=tt: E.tensor_reduce(
            qc(tt, 4), xt, AX.X, OP.max),
            reads=[f"xseg{tt}"], writes=[f"cQ_max{tt}"])
        S.add("vector", lambda E, xt=xt, tt=tt: E.tensor_reduce(
            qc(tt, 5), xt, AX.X, OP.min),
            reads=[f"xseg{tt}"], writes=[f"cQ_min{tt}"])
        S.add("vector", lambda E, tt=tt: E.tensor_scalar(
            qc(tt, 6), qc(tt, 2), 1.0 / D, None, OP.mult),
            reads=[f"cQ_sum{tt}"], writes=[f"cQ_m{tt}"])
        S.add("vector", lambda E, tt=tt: E.tensor_tensor(
            qc(tt, 7), qc(tt, 6), qc(tt, 6), OP.mult),
            reads=[f"cQ_m{tt}"], writes=[f"cQ_m2{tt}"])
        S.add("vector", lambda E, tt=tt: E.scalar_tensor_tensor(
            qc(tt, 8), qc(tt, 3), 1.0 / D, qc(tt, 7), OP.mult, OP.subtract),
            reads=[f"cQ_sq{tt}", f"cQ_m2{tt}"], writes=[f"cQ_var{tt}"])
        S.add("scalar", lambda E, tt=tt: E.activation(
            qc(tt, 9), qc(tt, 8), AF.Ln, bias=epscol),
            reads=[f"cQ_var{tt}", "epscol"], writes=[f"cQ_ln{tt}"])
        S.add("scalar", lambda E, tt=tt: E.activation(
            qc(tt, 10), qc(tt, 9), AF.Exp, scale=nhalfcol),
            reads=[f"cQ_ln{tt}", "nhalfcol"], writes=[f"cQ_rstd{tt}"])
        S.add("vector", lambda E, tt=tt: E.tensor_tensor(
            qc(tt, 11), qc(tt, 4), qc(tt, 6), OP.subtract),
            reads=[f"cQ_max{tt}", f"cQ_m{tt}"], writes=[f"cQ_a1{tt}"])
        S.add("vector", lambda E, tt=tt: E.tensor_tensor(
            qc(tt, 12), qc(tt, 6), qc(tt, 5), OP.subtract),
            reads=[f"cQ_m{tt}", f"cQ_min{tt}"], writes=[f"cQ_a2{tt}"])
        S.add("vector", lambda E, tt=tt: E.tensor_tensor(
            qc(tt, 11), qc(tt, 11), qc(tt, 12), OP.max),
            reads=[f"cQ_a1{tt}", f"cQ_a2{tt}"], writes=[f"cQ_a1{tt}"])
        S.add("vector", lambda E, tt=tt: E.scalar_tensor_tensor(
            qc(tt, 13), qc(tt, 11), QEPS, qc(tt, 10), OP.max, OP.mult),
            reads=[f"cQ_a1{tt}", f"cQ_rstd{tt}"], writes=[f"cQ_amx{tt}"])
        S.add("vector", lambda E, tt=tt: E.tensor_single_scalar(
            qc(tt, 13), qc(tt, 13), QEPS, OP.max),
            reads=[f"cQ_amx{tt}"], writes=[f"cQ_amx{tt}"])
        S.add("vector", lambda E, tt=tt: E.reciprocal(qc(tt, 14), qc(tt, 13)),
              reads=[f"cQ_amx{tt}"], writes=[f"cQ_ramx{tt}"])
        S.add("vector", lambda E, tt=tt: E.tensor_scalar(
            qc(tt, 15), qc(tt, 10), qc(tt, 14), 127.0, OP.mult, OP.mult),
            reads=[f"cQ_rstd{tt}", f"cQ_ramx{tt}"], writes=[f"qscl{tt}"])
        S.add("vector", lambda E, tt=tt: E.tensor_scalar(
            qc(tt, 7), qc(tt, 6), qc(tt, 15), -1.0, OP.mult, OP.mult),
            reads=[f"cQ_m{tt}", f"qscl{tt}"], writes=[f"qb2{tt}"])
        S.add("vector", lambda E, xt=xt, tt=tt: E.tensor_scalar(
            sqout, xt, qc(tt, 15), qc(tt, 7), OP.mult, OP.add),
            reads=[f"xseg{tt}", f"qscl{tt}", f"qb2{tt}"], writes=["sqout"])
        S.add("vector", lambda E: E.tensor_scalar(
            xn_tm[0], sqout, M64, M64, OP.add, OP.subtract),
            reads=["sqout"], writes=["xntm0"])
        S.add("vector", lambda E: E.tensor_scalar(
            sqout, sqout, M32, M32, OP.add, OP.subtract),
            reads=["sqout"], writes=["sqout"])
        S.add("vector", lambda E: E.scalar_tensor_tensor(
            xn_tm[1], sqout, dprime, xn_tm[0], OP.mult, OP.add),
            reads=["sqout", "dprime", "xntm0"], writes=["xntm1"])
        S.add("vector", lambda E, tt=tt: E.tensor_scalar(
            xn_tm[1], xn_tm[1], qc(tt, 13), 1.0 / 127.0, OP.mult, OP.mult),
            reads=["xntm1", f"cQ_amx{tt}"], writes=["xntm1"])
        for g in range(2):
            for i in range(4):
                kc = g * 4 + i
                S.add("tensor", lambda E, kc=kc, i=i: E.transpose(
                    psT[:, i * 256:i * 256 + 128],
                    xn_tm[1][:, kc * 128:(kc + 1) * 128], ident),
                    reads=["xntm1", "ident"], writes=["psT"])
            S.add("scalar", lambda E, s=s, tt=tt, g=g: E.copy(
                ap(xq_sb, g * 4 * T + s * 512 + tt * 128,
                   [[T, 4], [1, 128]]),
                ap(psT, 0, [[256, 4], [1, 128]])),
                reads=["psT"], writes=["xq"])
        if s + 1 < NSEG:
            x_load(s + 1, tt)
            ln1_reduce(s + 1, tt)

    for tt in range(4):
        x_load(0, tt)
    for tt in range(4):
        ln1_reduce(0, tt)

    for s in range(NSEG):
        ln1_cols_xn(s)
        qkv_proj(s)
        for tt in range(4):
            for bg in range(3):
                ncols = 396 if bg < 2 else 264
                S.add("scalar", lambda E, bg=bg, ncols=ncols: E.copy(
                    sbf3[bg][:, 0:ncols], psB[bg][:, 0:ncols]),
                    reads=[f"B{bg}"], writes=[f"sbf{bg}"])
            def stage1(P):
                b = PAIR_BANK[P]
                sl = PAIR_SLOT[P]
                yb = 3 if P % 2 == 0 else 4
                scb = 5 + (P % 2)
                qsl = phiQ[:, P * 512 + tt * 128:P * 512 + (tt + 1) * 128]
                ksl = phiK[:, P * 512 + tt * 128:P * 512 + (tt + 1) * 128]
                S.add("tensor", lambda E, qsl=qsl, ksl=ksl, scb=scb: E.matmul(
                    psB[scb][:, 0:128], ksl[0:64, :], qsl[0:64, :],
                    start=True, stop=False, skip_group_check=True),
                    reads=["phiK", "phiQ"], writes=[f"B{scb}"])
                S.add("tensor", lambda E, qsl=qsl, b=b, sl=sl, yb=yb: E.matmul(
                    psB[yb][:, 0:132], qsl, sbf3[b][:, sl * 132:(sl + 1) * 132],
                    start=True, stop=False, skip_group_check=True),
                    reads=["phiQ", f"sbf{b}"], writes=[f"B{yb}"])
                S.add("tensor", lambda E, qsl=qsl, ksl=ksl, scb=scb: E.matmul(
                    psB[scb][:, 128:256], ksl[64:128, :], qsl[64:128, :],
                    start=False, stop=True, skip_group_check=True),
                    reads=["phiK", "phiQ"], writes=[f"B{scb}"])
                S.add("tensor", lambda E, ksl=ksl, P=P: E.transpose(
                    psT[:, (P % 2) * 256:(P % 2) * 256 + 128], ksl, ident),
                    reads=["phiK", "ident"], writes=["psT"])
                S.add("scalar", lambda E, P=P: E.copy(
                    phiKT[P % 2], psT[:, (P % 2) * 256:(P % 2) * 256 + 128]),
                    reads=["psT"], writes=[f"phiKT{P % 2}"])

            def stage2(P):
                b = PAIR_BANK[P]
                sl = PAIR_SLOT[P]
                yb = 3 if P % 2 == 0 else 4
                scb = 5 + (P % 2)
                pp = P % 2
                S.add("vector", lambda E, pp=pp, scb=scb: E.tensor_tensor(
                    attnT[pp], psB[scb][:, 0:256], mask2, OP.mult),
                    reads=[f"B{scb}", "mask2"], writes=[f"attnT{pp}"])
                for h in range(2):
                    vs = vaug[:, tt * 1056 + P * 132 + h * 66:
                              tt * 1056 + P * 132 + h * 66 + 65]
                    S.add("tensor", lambda E, pp=pp, h=h, yb=yb, vs=vs: E.matmul(
                        psB[yb][:, h * 66:h * 66 + 65],
                        attnT[pp][:, h * 128:(h + 1) * 128], vs,
                        start=False, stop=(h == 1), skip_group_check=True),
                        reads=[f"attnT{pp}", "vaug"], writes=[f"B{yb}"])
                for h in range(2):
                    vs = vaug[:, tt * 1056 + P * 132 + h * 66:
                              tt * 1056 + P * 132 + h * 66 + 65]
                    stt = first_S_mm[b]
                    first_S_mm[b] = False
                    S.add("tensor", lambda E, pp=pp, h=h, b=b, sl=sl, vs=vs, stt=stt: E.matmul(
                        psB[b][h * 64:h * 64 + 64, sl * 132 + h * 66:sl * 132 + h * 66 + 65],
                        phiKT[pp][:, h * 64:(h + 1) * 64], vs,
                        start=stt, stop=False, skip_group_check=True,
                        tile_position=(0, h * 64)),
                        reads=[f"phiKT{pp}", "vaug"], writes=[f"B{b}"])
                S.add("vector", lambda E, pp=pp, yb=yb: E.tensor_single_scalar(
                    rdc[:, pp * 4:pp * 4 + 2],
                    ap(psB[yb], 64, [[66, 2]]), ZEPS, OP.max),
                    reads=[f"B{yb}"], writes=[f"rdc{pp}"])
                S.add("vector", lambda E, pp=pp: E.reciprocal(
                    rdc[:, pp * 4 + 2:pp * 4 + 4], rdc[:, pp * 4:pp * 4 + 2]),
                    reads=[f"rdc{pp}"], writes=[f"rdc{pp}"])
                for h in range(2):
                    S.add("vector", lambda E, pp=pp, yb=yb, h=h: E.tensor_scalar(
                        y_tm[pp][:, h * 64:(h + 1) * 64],
                        psB[yb][:, h * 66:h * 66 + 64],
                        rdc[:, pp * 4 + 2 + h:pp * 4 + 3 + h], None, OP.mult),
                        reads=[f"B{yb}", f"rdc{pp}"], writes=[f"ytm{pp}"])
                S.add("tensor", lambda E, pp=pp: E.transpose(
                    psT[:, 512 + pp * 256:512 + pp * 256 + 128], y_tm[pp], ident),
                    reads=[f"ytm{pp}", "ident"], writes=["psT"])
                S.add("scalar", lambda E, P=P, tt=tt, pp=pp: E.copy(
                    yT[:, P * 512 + tt * 128:P * 512 + (tt + 1) * 128],
                    psT[:, 512 + pp * 256:512 + pp * 256 + 128]),
                    reads=["psT"], writes=["xnT"])

            stage1(0)
            for P in range(1, 8):
                stage1(P)
                stage2(P - 1)
            stage2(7)
            emit_prepass(5)
            wo_and_tail(s, tt)
        emit_prepass(4)

    if t_xn is not None:
        S.add("sync", lambda E: E.dma_start(out=t_xn, in_=xnT),
              reads=["xnT"], dma=True)
    if t_phiq is not None:
        S.add("sync", lambda E: E.dma_start(out=t_phiq, in_=phiQ),
              reads=["phiQ"], dma=True)
    if t_phik is not None:
        S.add("sync", lambda E: E.dma_start(out=t_phik, in_=phiK),
              reads=["phiK"], dma=True)
    if t_vaug is not None:
        S.add("sync", lambda E: E.dma_start(out=bass.AP(
            t_vaug.tensor, 0, [[8 * 132, 128], [1, 4 * 8 * 132]]), in_=vaug),
            reads=["vaug"], dma=True)
    if t_y is not None:
        S.add("sync", lambda E: E.dma_start(out=t_y, in_=yT),
              reads=["xnT"], dma=True)
    if t_xq is not None:
        S.add("sync", lambda E: E.dma_start(out=t_xq, in_=xq_sb),
              reads=["xq"], dma=True)

    emit_prepass(192)
    emit_prepass_final()

    S.join()
    S.add("vector", lambda E: E.memset(maxacc, -3.0e38), writes=["maxacc"])
    S.add("vector", lambda E: E.memset(minacc, 3.0e38), writes=["minacc"])

    def wd_ternarize(jc):
        b = jc % 2
        S.add("gpsimd", lambda E, jc=jc, b=b: E.dma_start(
            out=wdbuf[b], in_=bass.AP(
                wd_d.tensor, jc * 128 * 1024, [[1024, 128], [1, 1024]])),
            writes=[f"wdbuf{b}"], dma=True)
        S.add("vector", lambda E, b=b: E.tensor_scalar(
            sqout, wdbuf[b], thr[:, 5:6], None, OP.is_le),
            reads=[f"wdbuf{b}", "thr"], writes=["sqout"])
        S.add("vector", lambda E, jc=jc, b=b: E.scalar_tensor_tensor(
            wdq_sb[:, jc * 1024:(jc + 1) * 1024], wdbuf[b], thr[:, 4:5],
            sqout, OP.is_ge, OP.subtract),
            reads=[f"wdbuf{b}", "thr", "sqout"], writes=["wdq"])

    for oc in range(32):
        wd_ternarize(oc)
        b = oc % 2
        S.add("sync", lambda E, oc=oc, b=b: E.dma_start(
            out=wstream[b][:, 0:1024], in_=bass.AP(
                wg_d.tensor, oc * 128 * 1024, [[1024, 128], [1, 1024]])),
            writes=[f"wstream{b}"], dma=True)
        S.add("sync", lambda E, oc=oc, b=b: E.dma_start(
            out=wstream[b][:, 1024:2048], in_=bass.AP(
                wu_d.tensor, oc * 128 * 1024, [[1024, 128], [1, 1024]])),
            writes=[f"wstream{b}"], dma=True)
        for (half, thri, wq2) in ((0, 0, wqg_sb), (1, 2, wqu_sb)):
            wsl = wstream[b][:, half * 1024:(half + 1) * 1024]
            S.add("vector", lambda E, wsl=wsl, thri=thri: E.tensor_scalar(
                sqout, wsl, thr[:, thri + 1:thri + 2], None, OP.is_le),
                reads=[f"wstream{b}", "thr"], writes=["sqout"])
            S.add("vector", lambda E, wsl=wsl, thri=thri, wq2=wq2, b=b: E.scalar_tensor_tensor(
                wq2[b], wsl, thr[:, thri:thri + 1], sqout, OP.is_ge, OP.subtract),
                reads=[f"wstream{b}", "thr", "sqout"], writes=[f"wq2_{id(wq2)}_{b}"])
        all_t8 = list(range(NSTRIP))
        t8_groups = [tuple(all_t8[i:i + 3]) for i in range(0, NSTRIP, 3)]
        for grp in t8_groups:
            ng = len(grp)
            gbanks = [0, 1, 2][:ng]
            ubanks = [3, 4, 5][:ng]
            for kc in range(8):
                for gi, t8 in enumerate(grp):
                    S.add("tensor", lambda E, kc=kc, b=b, gb=gbanks[gi], t8=t8: E.matmul(
                        psB[gb], wqg_sb[b][:, kc * 128:(kc + 1) * 128],
                        xq_sb[:, kc * T + t8 * 512:kc * T + (t8 + 1) * 512],
                        start=(kc == 0), stop=(kc == 7)),
                        reads=[f"wq2_{id(wqg_sb)}_{b}", "xq"], writes=[f"B{gbanks[gi]}"])
            for kc in range(8):
                for gi, t8 in enumerate(grp):
                    S.add("tensor", lambda E, kc=kc, b=b, ub=ubanks[gi], t8=t8: E.matmul(
                        psB[ub], wqu_sb[b][:, kc * 128:(kc + 1) * 128],
                        xq_sb[:, kc * T + t8 * 512:kc * T + (t8 + 1) * 512],
                        start=(kc == 0), stop=(kc == 7)),
                        reads=[f"wq2_{id(wqu_sb)}_{b}", "xq"], writes=[f"B{ubanks[gi]}"])
            for gi, t8 in enumerate(grp):
                gb, ub = gbanks[gi], ubanks[gi]
                pp = t8 % 2
                S.add("scalar", lambda E, gb=gb, pp=pp: E.activation(
                    g2_sb[pp], psB[gb], AF.Silu, scale=mw[:, 0:1]),
                    reads=[f"B{gb}", "mw"], writes=[f"g2_{pp}"])
                S.add("scalar", lambda E, ub=ub, pp=pp: E.copy(ud_sb[pp], psB[ub]),
                      reads=[f"B{ub}"], writes=[f"ud_{pp}"])
                S.add("vector", lambda E, pp=pp: E.tensor_tensor(
                    hbf_sb[pp], g2_sb[pp], ud_sb[pp], OP.mult),
                    reads=[f"g2_{pp}", f"ud_{pp}"], writes=[f"hbf{pp}"])
                S.add("vector", lambda E, pp=pp, t8=t8: E.tensor_tensor(
                    maxacc[:, t8 * 512:(t8 + 1) * 512],
                    maxacc[:, t8 * 512:(t8 + 1) * 512], hbf_sb[pp], OP.max),
                    reads=[f"hbf{pp}", "maxacc"], writes=["maxacc"])
                S.add("vector", lambda E, pp=pp, t8=t8: E.tensor_tensor(
                    minacc[:, t8 * 512:(t8 + 1) * 512],
                    minacc[:, t8 * 512:(t8 + 1) * 512], hbf_sb[pp], OP.min),
                    reads=[f"hbf{pp}", "minacc"], writes=["minacc"])
                for hh in range(2):
                    S.add("sync", lambda E, oc=oc, t8=t8, pp=pp, hh=hh: E.dma_start(
                        out=bass.AP(h_d.tensor,
                                    ((2 * t8 + hh) * INNER + oc * 128) * 256,
                                    [[256, 128], [1, 256]]),
                        in_=hbf_sb[pp][:, hh * 256:(hh + 1) * 256]),
                        reads=[f"hbf{pp}"], dma=True)
    S.add("vector", lambda E: E.tensor_single_scalar(minacc, minacc, -1.0, OP.mult),
          reads=["minacc"], writes=["minacc"])
    S.add("vector", lambda E: E.tensor_tensor(mxabs, maxacc, minacc, OP.max),
          reads=["maxacc", "minacc"], writes=["mxabs"])
    for c in range(NT128):
        sl = c % 4
        S.add("tensor", lambda E, c=c, sl=sl: E.transpose(
            psT[:, sl * 256:sl * 256 + 128], mxabs[:, c * 128:(c + 1) * 128], ident),
            reads=["mxabs", "ident"], writes=["psT"])
        S.add("vector", lambda E, c=c, sl=sl: E.tensor_reduce(
            mxw[:, c:c + 1], psT[:, sl * 256:sl * 256 + 128], AX.X, OP.max),
            reads=["psT"], writes=["mxw"])
    S.add("vector", lambda E: E.tensor_scalar(
        betac, mxw, mw[:, 1:2], QEPS, OP.mult, OP.max),
        reads=["mxw", "mw"], writes=["acl"])
    S.add("vector", lambda E: E.reciprocal(mxw, betac),
          reads=["acl"], writes=["rac"])
    S.add("vector", lambda E: E.tensor_scalar(
        mxw, mxw, mw[:, 1:2], 127.0, OP.mult, OP.mult),
        reads=["rac", "mw"], writes=["qscols"])
    S.add("vector", lambda E: E.tensor_scalar(
        betac, betac, mw[:, 2:3], 1.0 / 127.0, OP.mult, OP.mult),
        reads=["acl", "mw"], writes=["betac"])

    if t_wdq is not None:
        S.add("sync", lambda E: E.dma_start(out=t_wdq, in_=wdq_sb),
              reads=["wdq"], dma=True)
    if t_mx is not None:
        S.add("sync", lambda E: E.dma_start(out=t_mx, in_=mxw),
              reads=["qscols"], dma=True)

    S.join()

    def b2_strip_load(hs):
        buf = hs % 2
        for g in range(2):
            S.add("sync", lambda E, hs=hs, g=g, buf=buf: E.dma_start(
                out=ap(hstrip[buf], g * 16 * 256, [[256, 16], [1, 256]]),
                in_=bass.AP(h_d.tensor, (hs * INNER + g * 16 * 128) * 256,
                            [[256, 128], [128 * 256, 16], [1, 256]])),
                writes=[f"hstrip{buf}"], dma=True)

    def b2_quant(tc):
        c = tc % 2
        hs = tc // 2
        buf = hs % 2
        hq = hq_sb[tc % 2]
        S.add("tensor", lambda E, tc=tc: E.transpose(
            bass.AP(psB[6].tensor, psB[6].offset, [[512, 1], [1, 128]]),
            mxw[:, tc:tc + 1], identf),
            reads=["qscols", "identf"], writes=["B6"])
        S.add("scalar", lambda E, tc=tc: E.copy(
            qrow1[tc % 2], bass.AP(psB[6].tensor, psB[6].offset, [[512, 1], [1, 128]])),
            reads=["B6"], writes=[f"qrow1{tc % 2}"])
        S.add("tensor", lambda E, tc=tc: E.matmul(
            psB[6][:, 0:128], onerow_f, qrow1[tc % 2], start=True, stop=True),
            reads=["onerow", f"qrow1{tc % 2}"], writes=["B6"])
        S.add("scalar", lambda E, tc=tc: E.copy(qsb[tc % 2], psB[6][:, 0:128]),
              reads=["B6"], writes=[f"qsb{tc % 2}"])
        for qq in range(4):
            hsl = ap(hstrip[buf], qq * 8 * 256 + c * 128, [[256, 8], [1, 128]])
            S.add("gpsimd", lambda E, hsl=hsl, tc=tc: E.tensor_tensor(
                ap(qy, 0, [[128, 8], [1, 128]]), hsl,
                ap(qsb[tc % 2], 0, [[0, 8], [1, 128]]), OP.mult),
                reads=[f"hstrip{buf}", f"qsb{tc % 2}"], writes=["qy"])
            S.add("vector", lambda E: E.tensor_scalar(
                qC, qy, M64, M64, OP.add, OP.subtract),
                reads=["qy"], writes=["qC"])
            S.add("vector", lambda E: E.tensor_scalar(
                qy, qy, M32, M32, OP.add, OP.subtract),
                reads=["qy"], writes=["qy"])
            S.add("vector", lambda E, hq=hq, qq=qq, tc=tc: E.scalar_tensor_tensor(
                hq[:, qq * 1024:(qq + 1) * 1024], qy, dprime, qC,
                OP.mult, OP.add),
                reads=["qy", "dprime", "qC"], writes=[f"hq{tc % 2}"])
        S.add("sync", lambda E, tc=tc: E.dma_start(
            out=x1ld[tc % 2], in_=bass.AP(x1_d.tensor, tc * 128 * D,
                                          [[D, 128], [1, D]])),
            writes=[f"x1ld{tc % 2}"], dma=True)

    def b2_mm(tc):
        hq = hq_sb[tc % 2]
        bA, bB = (3, 4) if tc % 2 == 0 else (5, 2)
        for jo in range(32):
            for oh, bank in ((0, bA), (1, bB)):
                S.add("tensor", lambda E, jo=jo, oh=oh, bank=bank, hq=hq, tc=tc: E.matmul(
                    psB[bank], hq[:, jo * 128:(jo + 1) * 128],
                    wdq_sb[:, jo * D + oh * 512:jo * D + (oh + 1) * 512],
                    start=(jo == 0), stop=(jo == 31)),
                    reads=[f"hq{tc % 2}", "wdq"], writes=[f"B{bank}"])
        for oh, bank in ((0, bA), (1, bB)):
            S.add("vector", lambda E, oh=oh, bank=bank, tc=tc: E.scalar_tensor_tensor(
                outt[tc % 2][:, oh * 512:(oh + 1) * 512], psB[bank],
                betac[:, tc:tc + 1], x1ld[tc % 2][:, oh * 512:(oh + 1) * 512],
                OP.mult, OP.add),
                reads=[f"B{bank}", "betac", f"x1ld{tc % 2}"],
                writes=[f"outt{tc % 2}"])
        S.add("sync", lambda E, tc=tc: E.dma_start(
            out=bass.AP(out_d.tensor, tc * 128 * D, [[D, 128], [1, D]]),
            in_=outt[tc % 2]),
            reads=[f"outt{tc % 2}"], dma=True)

    b2_strip_load(0)
    b2_strip_load(1)
    for tc in range(NT128):
        b2_quant(tc)
        if tc % 2 == 1 and tc // 2 + 2 < NHSTRIP:
            b2_strip_load(tc // 2 + 2)
        if tc >= 1:
            b2_mm(tc - 1)
    b2_mm(NT128 - 1)

    fin_waits = [(c, S.ctr[c]) for c in DMA_LANES if S.ctr[c] > 0]
    S.add("vector", lambda E: E.memset(redcol, 0.0),
          waits=fin_waits, writes=["redcol"])

    S.emit(nc, sems)
    ctx.close()
    return nc, taps


def _host_layouts(W_q, W_k, W_v, W_o, w_gate, w_up, w_down):
    def attw(W):
        return np.ascontiguousarray(
            W.T.reshape(8, 128, 1024).transpose(1, 0, 2).reshape(128, 8192))

    def ffnw(W):
        g = W.T.reshape(8, 128, 32, 128)
        return np.ascontiguousarray(g.transpose(2, 1, 0, 3).reshape(32, 128, 1024))

    def dw(W):
        return np.ascontiguousarray(W.T.reshape(32, 128, 1024))

    return (attw(W_q), attw(W_k), attw(W_v), attw(W_o),
            ffnw(w_gate), ffnw(w_up), dw(w_down))


_NC_CACHE = {}


def kernel(L0_attn_x, L0_attn_W_q, L0_attn_W_k, L0_attn_W_v, L0_attn_W_o,
           L0_ffn_w_gate, L0_ffn_w_up, L0_ffn_w_down):
    x = np.asarray(L0_attn_x, dtype=np.float32)
    B, T, _ = x.shape
    wq, wk, wv, wo, wg, wu, wd = _host_layouts(
        np.asarray(L0_attn_W_q, np.float32), np.asarray(L0_attn_W_k, np.float32),
        np.asarray(L0_attn_W_v, np.float32), np.asarray(L0_attn_W_o, np.float32),
        np.asarray(L0_ffn_w_gate, np.float32), np.asarray(L0_ffn_w_up, np.float32),
        np.asarray(L0_ffn_w_down, np.float32))
    key = T
    if key not in _NC_CACHE:
        _NC_CACHE[key] = build_program(T=T)[0]
    nc = _NC_CACHE[key]
    in_maps = []
    for c in range(B):
        in_maps.append({
            "x": np.ascontiguousarray(x[c]),
            "wqT": wq, "wkT": wk, "wvT": wv, "woT": wo,
            "wgT": wg, "wuT": wu, "wdT": wd,
        })
    res = run_bass_kernel_spmd(nc, in_maps, core_ids=list(range(B)))
    return np.stack([res.results[c]["out"] for c in range(B)], axis=0)
